# version 01
# baseline (speedup 1.0000x reference)
import contextlib
import os
import numpy as np

import concourse.bass as bass
import concourse.mybir as mybir
from concourse.bass_utils import run_bass_kernel_spmd

F32 = mybir.dt.float32
BF16 = mybir.dt.bfloat16
FP8 = mybir.dt.float8e4
AF = mybir.ActivationFunctionType
ALU = mybir.AluOpType
AX = mybir.AxisListType

NCORE = 8
T = 2048
D = 2048
NT = 16
KC = 16
H = 16
EPS_LN = 1e-5
EPS_Z = 1e-6
FP8_HALF_SCALE = 179.2
DEQ_CONST = 2.0 / (358.4 * 358.4)

ENGS = ("sp", "act", "dve", "pe", "po")

ST1 = 0
STW = 7
ST2 = 8
ST3 = 16


class Sem:
    def __init__(self, handle, name):
        self.h = handle
        self.name = name
        self.n = 0


class Prog:
    def __init__(self):
        self.ops = {e: [] for e in ENGS}
        self.waited = {e: {} for e in ENGS}

    def op(self, eng, fn, waits=(), inc=None, amt=1):
        real = []
        for wv in waits:
            if wv is None:
                continue
            s, v = wv
            if v is None:
                continue
            w = self.waited[eng]
            if w.get(s.name, -1) >= v:
                continue
            w[s.name] = v
            real.append((s.h, v))
        post = None
        if inc is not None:
            inc.n += amt
            post = inc.n
        self.ops[eng].append((real, fn, (inc.h, amt) if inc else None))
        return post


def build_program(upto=3, dbg=False, astage=99):
    nc = bass.Bass("TRN2", target_bir_lowering=False)

    x0_d = nc.dram_tensor("x0", [T, D], F32, kind="ExternalInput").ap()
    wq_d = nc.dram_tensor("wq", [D, D], F32, kind="ExternalInput").ap()
    wk_d = nc.dram_tensor("wk", [D, D], F32, kind="ExternalInput").ap()
    wv_d = nc.dram_tensor("wv", [D, D], F32, kind="ExternalInput").ap()
    wo_d = nc.dram_tensor("wo", [D, D], F32, kind="ExternalInput").ap()
    wt_d = nc.dram_tensor("wt", [D, D], F32, kind="ExternalInput").ap()
    bq_d = nc.dram_tensor("bq_cols", [128, KC], F32, kind="ExternalInput").ap()
    bk_d = nc.dram_tensor("bk_cols", [128, KC], F32, kind="ExternalInput").ap()
    bv_d = nc.dram_tensor("bv_row", [1, D], F32, kind="ExternalInput").ap()
    bo_d = nc.dram_tensor("bo_row", [1, D], F32, kind="ExternalInput").ap()
    bf_d = nc.dram_tensor("bf_row", [1, D], F32, kind="ExternalInput").ap()
    mask_d = nc.dram_tensor("mask4", [128, 512], BF16, kind="ExternalInput").ap()
    ident_d = nc.dram_tensor("ident", [128, 128], BF16, kind="ExternalInput").ap()
    ones_d = nc.dram_tensor("ones_row", [1, 128], F32, kind="ExternalInput").ap()
    oddf_d = nc.dram_tensor("oddflag", [128, 1], F32, kind="ExternalInput").ap()
    evenf_d = nc.dram_tensor("evenflag", [128, 1], F32, kind="ExternalInput").ap()
    out_d = nc.dram_tensor("out", [T, D], F32, kind="ExternalOutput").ap()

    X1_d = nc.dram_tensor("X1s", [T, D], F32).ap()
    X2_d = nc.dram_tensor("X2s", [T, D], F32).ap()
    U2_d = nc.dram_tensor("U2s", [T, D], BF16).ap()
    UQ_d = nc.dram_tensor("UQs", [T, D], BF16).ap()
    QF_d = nc.dram_tensor("QFs", [D, T], BF16).ap()
    KF_d = nc.dram_tensor("KFs", [D, T], BF16).ap()
    VT_d = nc.dram_tensor("VTs", [NT, 4, 128, 516], BF16).ap()
    SCR_ds = [nc.dram_tensor("SCRs%d" % i, [128, 16], F32).ap() for i in range(3)]
    cc_am_in = nc.dram_tensor("cc_am_in", [1, 8], F32)
    cc_am_out = nc.dram_tensor("cc_am_out", [1, 8], F32)
    cc_am_in2 = nc.dram_tensor("cc_am_in2", [1, 8], F32)
    cc_am_out2 = nc.dram_tensor("cc_am_out2", [1, 8], F32)
    cc_kv_in = nc.dram_tensor("cc_kv_in", [H * 128, 129], F32)
    cc_kv_out = nc.dram_tensor("cc_kv_out", [H * 128, 129], F32)

    class _StageStopCls(Exception):
        pass
    _StageStop = _StageStopCls()

    p = Prog()
    es = contextlib.ExitStack()
    SB = lambda name, shape, dt: es.enter_context(nc.sbuf_tensor(name, shape, dt))

    A1 = SB("A1", [128, KC, D], BF16)
    A2 = SB("A2", [128, KC, D], BF16)
    POOL = SB("POOL", [128, 30768], BF16)
    s_st = SB("s_st", [128, H, 129], F32)
    s_stats = SB("s_stats", [128, 24, KC], F32)
    s_mask = SB("s_mask", [128, 512], BF16)
    s_ident = SB("s_ident", [128, 128], BF16)
    s_cols = SB("s_cols", [128, 4, KC], F32)
    s_bc = SB("s_bc", [128, 8], F32)
    s_flag = SB("s_flag", [128, 2], F32)
    s_rowb = SB("s_rowb", [1, 1, D], BF16)
    s_ones = SB("s_ones", [1, 128], F32)
    s_onesb = SB("s_onesb", [1, 128], BF16)
    s_onescol = SB("s_onescol", [128, 1], BF16)
    s_sc = SB("s_sc", [1, 32], F32)
    s_small = SB("s_small", [128, 40], F32)

    pb = [es.enter_context(nc.psum_tensor("pb%d" % i, [128, 512], F32)) for i in range(7)]
    pb7 = es.enter_context(nc.psum_tensor("pb7", [128, 1024], BF16))

    sems = {}
    for name in ("ld", "gp", "pe", "ve", "ac", "po", "c1", "c2", "c3"):
        sems[name] = Sem(es.enter_context(nc.semaphore(name)), name)
    s_ld, s_gp, s_pe, s_ve, s_ac = (sems[k] for k in ("ld", "gp", "pe", "ve", "ac"))
    s_c1, s_c2, s_c3 = sems["c1"], sems["c2"], sems["c3"]

    bank_free = [None] * 8
    bank_ready = [None] * 8

    PF = POOL[:].bitcast(F32)
    FPv = POOL[:].bitcast(FP8)

    def pool_f32(off, n):
        return PF[:, off // 2: off // 2 + n]

    def pool_bf16(off, n):
        return POOL[:, off: off + n]

    def pool_fp8(off, n):
        return FPv[:, off * 2: off * 2 + n]

    def a1_t(kc):
        return A1[:, kc, :]

    def a2_t(kc):
        return A2[:, kc, :]

    def dma(eng, out_ap, in_ap, waits=(), transpose=False):
        s = s_ld if eng == "sp" else s_gp
        eng = "sp" if eng == "sp" else "po"
        if transpose:
            fn = (lambda e, o=out_ap, i=in_ap: e.dma_start_transpose(o, i))
        else:
            fn = (lambda e, o=out_ap, i=in_ap: e.dma_start(out=o, in_=i))
        v = p.op(eng, fn, waits=waits, inc=s, amt=16)
        return (s, v)

    def scalarize(src_ap, ncols, scr_idx, dst_ap, waits, row_scratch):
        scr = SCR_ds[scr_idx]
        w1 = dma("gp", scr[:, 0:ncols], src_ap, waits=waits)
        w2 = dma("gp", row_scratch[0:1, 0:128 * ncols],
                 scr.rearrange("p n -> (p n)")[None, :],
                 waits=[w1])
        v = p.op("dve", lambda e, d=dst_ap, s=row_scratch, n=128 * ncols:
                 e.tensor_reduce(d, s[0:1, 0:n], AX.X, ALU.max),
                 waits=[w2], inc=s_ve)
        return (s_ve, v)

    def bcast(src_ap, dst_ap, n, waits, bank=6):
        wf = bank_free[bank]
        v = p.op("pe", lambda e, s=src_ap, n=n, b=bank: e.matmul(
            pb[b][:, 0:n], s_ones[:], s, start=True, stop=True),
            waits=list(waits) + ([wf] if wf else []), inc=s_pe)
        bank_ready[bank] = (s_pe, v)
        v2 = p.op("dve", lambda e, d=dst_ap, n=n, b=bank: e.tensor_copy(d, pb[b][:, 0:n]),
                  waits=[(s_pe, v)], inc=s_ve)
        bank_free[bank] = (s_ve, v2)
        return (s_ve, v2)

    def stats_tile(tile_ap, i, base, waits, scratch_bf16, do_amax):
        p.op("act", lambda e, t=tile_ap, i=i, b=base, s=scratch_bf16: e.activation(
            s, t, AF.Square, accum_out=s_stats[:, b + 1, i:i + 1]), waits=waits)
        va = p.op("act", lambda e, t=tile_ap, i=i, b=base, s=scratch_bf16: e.activation(
            s, t, AF.Identity, accum_out=s_stats[:, b + 0, i:i + 1]), inc=s_ac)
        vd = None
        if do_amax:
            p.op("dve", lambda e, t=tile_ap, i=i, b=base: e.tensor_reduce(
                s_stats[:, b + 2, i:i + 1], t, AX.X, ALU.max), waits=waits)
            vd = p.op("dve", lambda e, t=tile_ap, i=i, b=base: e.tensor_reduce(
                s_stats[:, b + 3, i:i + 1], t, AX.X, ALU.min), inc=s_ve)
        return (s_ac, va), ((s_ve, vd) if vd is not None else None)

    def mr_cols(base, sl, waits):
        n = sl.stop - sl.start
        v1 = p.op("dve", lambda e, b=base, sl=sl: e.tensor_scalar(
            out=s_stats[:, b + 4, sl], in0=s_stats[:, b + 0, sl],
            scalar1=1.0 / D, scalar2=None, op0=ALU.mult), waits=waits, inc=s_ve)
        v2 = p.op("dve", lambda e, b=base, sl=sl: e.tensor_scalar(
            out=s_stats[:, b + 1, sl], in0=s_stats[:, b + 1, sl],
            scalar1=1.0 / D, scalar2=None, op0=ALU.mult), inc=s_ve)
        v3 = p.op("dve", lambda e, b=base, sl=sl, n=n: e.tensor_tensor(
            s_small[:, 0:n], s_stats[:, b + 4, sl], s_stats[:, b + 4, sl], ALU.mult),
            waits=[(s_ve, v1)], inc=s_ve)
        vv = p.op("dve", lambda e, b=base, sl=sl, n=n: e.tensor_tensor(
            s_stats[:, b + 1, sl], s_stats[:, b + 1, sl], s_small[:, 0:n], ALU.subtract),
            waits=[(s_ve, v2), (s_ve, v3)], inc=s_ve)
        vl = p.op("act", lambda e, b=base, sl=sl: e.activation(
            s_stats[:, b + 5, sl], s_stats[:, b + 1, sl], AF.Ln, bias=np.float32(EPS_LN), scale=1.0),
            waits=[(s_ve, vv)], inc=s_ac)
        va = p.op("act", lambda e, b=base, sl=sl: e.activation(
            s_stats[:, b + 5, sl], s_stats[:, b + 5, sl], AF.Exp, scale=-0.5),
            waits=[(s_ac, vl)], inc=s_ac)
        return (s_ac, va)

    def amax_cols(base, sl, waits):
        n = sl.stop - sl.start
        v1 = p.op("dve", lambda e, b=base, sl=sl, n=n: e.tensor_tensor(
            s_small[:, 0:n], s_stats[:, b + 2, sl], s_stats[:, b + 4, sl], ALU.subtract),
            waits=waits, inc=s_ve)
        v2 = p.op("dve", lambda e, b=base, sl=sl, n=n: e.tensor_tensor(
            s_small[:, 16:16 + n], s_stats[:, b + 4, sl], s_stats[:, b + 3, sl], ALU.subtract),
            inc=s_ve)
        v3 = p.op("dve", lambda e, b=base, sl=sl, n=n: e.tensor_tensor(
            s_small[:, 0:n], s_small[:, 0:n], s_small[:, 16:16 + n], ALU.max),
            waits=[(s_ve, v1), (s_ve, v2)], inc=s_ve)
        vv = p.op("dve", lambda e, b=base, sl=sl, n=n: e.tensor_tensor(
            s_stats[:, b + 6, sl], s_small[:, 0:n], s_stats[:, b + 5, sl], ALU.mult),
            waits=[(s_ve, v3)], inc=s_ve)
        return (s_ve, vv)

    def mr_tile(base, i, waits):
        v1 = p.op("dve", lambda e, i=i, b=base: e.tensor_scalar(
            out=s_stats[:, b + 4, i:i + 1], in0=s_stats[:, b + 0, i:i + 1],
            scalar1=1.0 / D, scalar2=None, op0=ALU.mult), waits=waits, inc=s_ve)
        v2 = p.op("dve", lambda e, i=i, b=base: e.tensor_scalar(
            out=s_stats[:, b + 1, i:i + 1], in0=s_stats[:, b + 1, i:i + 1],
            scalar1=1.0 / D, scalar2=None, op0=ALU.mult), inc=s_ve)
        v3 = p.op("dve", lambda e, i=i, b=base: e.tensor_tensor(
            s_small[:, 36:37], s_stats[:, b + 4, i:i + 1],
            s_stats[:, b + 4, i:i + 1], ALU.mult), waits=[(s_ve, v1)], inc=s_ve)
        vv = p.op("dve", lambda e, i=i, b=base: e.tensor_tensor(
            s_stats[:, b + 1, i:i + 1], s_stats[:, b + 1, i:i + 1],
            s_small[:, 36:37], ALU.subtract), waits=[(s_ve, v2), (s_ve, v3)], inc=s_ve)
        vl = p.op("act", lambda e, i=i, b=base: e.activation(
            s_stats[:, b + 5, i:i + 1], s_stats[:, b + 1, i:i + 1],
            AF.Ln, bias=np.float32(EPS_LN), scale=1.0), waits=[(s_ve, vv)], inc=s_ac)
        va = p.op("act", lambda e, i=i, b=base: e.activation(
            s_stats[:, b + 5, i:i + 1], s_stats[:, b + 5, i:i + 1],
            AF.Exp, scale=-0.5), waits=[(s_ac, vl)], inc=s_ac)
        vneg = p.op("dve", lambda e, i=i, b=base: e.scalar_tensor_tensor(
            out=s_small[:, 34 + (i % 2):35 + (i % 2)], in0=s_stats[:, b + 4, i:i + 1],
            scalar=-1.0, in1=s_stats[:, b + 5, i:i + 1], op0=ALU.mult, op1=ALU.mult),
            waits=[(s_ac, va)], inc=s_ve)
        return (s_ve, vneg)

    dma("gp", s_cols[:, 0, :], bq_d)
    dma("gp", s_cols[:, 2, :], bk_d)
    dma("gp", s_mask[:], mask_d)
    dma("gp", s_ident[:], ident_d)
    dma("gp", s_ones[:], ones_d)
    dma("gp", s_flag[:, 0:1], oddf_d)
    dma("gp", s_flag[:, 1:2], evenf_d)
    dma("gp", s_rowb[0:1, 0, :], bo_d)
    w_c0 = (s_gp, s_gp.n)
    p.op("dve", lambda e: e.tensor_scalar(out=s_cols[:, 1, :], in0=s_cols[:, 0, :],
                                          scalar1=-1.0, scalar2=None, op0=ALU.mult),
         waits=[w_c0])
    p.op("dve", lambda e: e.tensor_scalar(out=s_cols[:, 3, :], in0=s_cols[:, 2, :],
                                          scalar1=-1.0, scalar2=None, op0=ALU.mult))
    p.op("dve", lambda e: e.memset(s_onescol[:], 1.0))
    v_c = p.op("dve", lambda e: e.tensor_copy(s_onesb[:], s_ones[:]), inc=s_ve)
    w_const = (s_ve, v_c)

    X1A = [0, 4096, 8192]
    SCR_BF = 12288
    W1A = [16384, 20480]
    ROWS = 24576
    UQT = [12288, 16384]
    X1ST = 20480
    U2ST = 24576
    Q8A = [26624, 27648]
    FROW = 28672

    xfree = [None, None, None]
    last_st = []
    for i in range(NT):
        j = i % 3
        w = dma("sp", pool_f32(X1A[j], 2048), x0_d[i * 128:(i + 1) * 128, :],
                waits=[xfree[j]] if xfree[j] else [])
        (wa, wv) = stats_tile(pool_f32(X1A[j], 2048), i, ST1, [w],
                              pool_bf16(SCR_BF, 2048), True)
        xfree[j] = wv
        last_st = [wa, wv]
    wfree = [None, None]
    vwa = None
    for i in range(NT):
        j = i % 2
        w = dma("gp", pool_f32(W1A[j], 2048), wt_d[i * 128:(i + 1) * 128, :],
                waits=[wfree[j]] if wfree[j] else [])
        vwa = p.op("dve", lambda e, j=j, i=i: e.tensor_reduce(
            s_stats[:, STW, i:i + 1], pool_f32(W1A[j], 2048), AX.X, ALU.max,
            apply_absolute_value=True), waits=[w], inc=s_ve)
        wfree[j] = (s_ve, vwa)
    w_wpass1 = (s_ve, vwa)

    w_mr1 = mr_cols(ST1, slice(0, NT), last_st)
    w_am1 = amax_cols(ST1, slice(0, NT), [w_mr1, last_st[1]])
    row_scr = pool_f32(ROWS, 2048)
    w_ax = scalarize(s_stats[:, ST1 + 6, :], NT, 0, s_sc[0:1, 0:1], [w_am1], row_scr)
    w_aw = scalarize(s_stats[:, STW, :], NT, 1, s_sc[0:1, 1:2], [w_wpass1], row_scr)

    w1 = dma("gp", cc_am_in.ap()[0:1, 0:1], s_sc[0:1, 0:1], waits=[w_ax])
    p.op("po", lambda e: e.collective_compute(
        "AllReduce", ALU.max, replica_groups=[list(range(NCORE))],
        ins=[cc_am_in.ap().opt()], outs=[cc_am_out.ap().opt()]),
        waits=[w1], inc=s_c1)
    w2 = dma("gp", s_sc[0:1, 0:1], cc_am_out.ap()[0:1, 0:1], waits=[(s_c1, 1)])

    def ffn_scales(w_amax_ready, row_free_wait):
        wbf = dma("gp", pool_f32(ROWS, 2048)[0:1, :], bf_d,
                  waits=list(row_free_wait) if row_free_wait else [])
        v0 = p.op("dve", lambda e: e.tensor_tensor(s_sc[0:1, 6:7], s_sc[0:1, 0:1],
                                                   s_sc[0:1, 1:2], ALU.max),
                  waits=w_amax_ready, inc=s_ve)
        va1 = p.op("dve", lambda e: e.reciprocal(s_sc[0:1, 2:3], s_sc[0:1, 0:1]), inc=s_ve)
        va2 = p.op("dve", lambda e: e.reciprocal(s_sc[0:1, 3:4], s_sc[0:1, 6:7]),
                   waits=[(s_ve, v0)], inc=s_ve)
        va3 = p.op("dve", lambda e: e.tensor_scalar(out=s_sc[0:1, 2:3], in0=s_sc[0:1, 2:3],
                                                    scalar1=FP8_HALF_SCALE, scalar2=None, op0=ALU.mult),
                   waits=[(s_ve, va1)], inc=s_ve)
        va4 = p.op("dve", lambda e: e.tensor_scalar(out=s_sc[0:1, 3:4], in0=s_sc[0:1, 3:4],
                                                    scalar1=FP8_HALF_SCALE, scalar2=None, op0=ALU.mult),
                   waits=[(s_ve, va2)], inc=s_ve)
        va5 = p.op("dve", lambda e: e.tensor_tensor(s_sc[0:1, 4:5], s_sc[0:1, 0:1],
                                                    s_sc[0:1, 6:7], ALU.mult),
                   waits=[(s_ve, v0)], inc=s_ve)
        va6 = p.op("dve", lambda e: e.tensor_scalar(out=s_sc[0:1, 4:5], in0=s_sc[0:1, 4:5],
                                                    scalar1=DEQ_CONST, scalar2=None, op0=ALU.mult),
                   waits=[(s_ve, va5)], inc=s_ve)
        v = p.op("dve", lambda e: e.reciprocal(s_sc[0:1, 5:6], s_sc[0:1, 4:5]),
                 waits=[(s_ve, va6)], inc=s_ve)
        vrow = p.op("dve", lambda e: e.tensor_scalar(
            out=pool_f32(ROWS, 2048)[0:1, :], in0=pool_f32(ROWS, 2048)[0:1, :],
            scalar1=s_sc[0:1, 5:6], scalar2=0.5, op0=ALU.mult, op1=ALU.mult),
            waits=[(s_ve, v), wbf], inc=s_ve)
        vb = p.op("dve", lambda e: e.tensor_copy(pool_bf16(FROW, 2048)[0:1, :], pool_f32(ROWS, 2048)[0:1, :]),
                  waits=[(s_ve, vrow)], inc=s_ve)
        wq1 = bcast(s_sc[0:1, 2:3], s_bc[:, 0:1], 1, [(s_ve, vb), (s_ve, va3), (s_ve, va4)])
        wq2 = bcast(s_sc[0:1, 4:5], s_bc[:, 1:2], 1, [wq1])
        wq3 = bcast(s_sc[0:1, 3:4], s_bc[:, 2:3], 1, [wq2])
        return wq3

    w_sc1 = ffn_scales([w2, w_aw, w_const], None)

    def w_quant_pass(w_scale_ready, gate, a1_war):
        wfree2 = [gate, gate]
        qact = [None, None]
        last = None
        for i in range(NT):
            j = i % 2
            w = dma("sp", pool_f32(W1A[j], 2048), wt_d[i * 128:(i + 1) * 128, :],
                    waits=[wfree2[j]] if wfree2[j] else [])
            vq = p.op("dve", lambda e, j=j: e.tensor_scalar(
                out=pool_fp8(Q8A[j], 2048), in0=pool_f32(W1A[j], 2048),
                scalar1=s_bc[:, 2:3], scalar2=None, op0=ALU.mult),
                waits=[w, w_scale_ready] + ([qact[j]] if qact[j] else []), inc=s_ve)
            wfree2[j] = (s_ve, vq)
            last = p.op("act", lambda e, i=i, j=j: e.copy(a1_t(i), pool_fp8(Q8A[j], 2048)),
                        waits=[(s_ve, vq)] + ([a1_war] if a1_war else []), inc=s_ac)
            qact[j] = (s_ac, last)
        return (s_ac, last)

    w_wq = w_quant_pass(w_sc1, w_wpass1, None)

    def ffn_pass(src_d, dst_d, stats_base, out_stats_base, w_wq_ready, w_bc_ready,
                 produce_u2, src_ready, u2_dst=None):
        xfree3 = [src_ready, src_ready, src_ready]
        uqt_free = [None, None]
        x1_free = None
        u2_free = None
        q8_free = [None, None]
        w_last = None
        for g in range(8):
            tiles = [g * 2, g * 2 + 1]
            uq_written = []
            xslot = {}
            for i in tiles:
                j = i % 3
                xslot[i] = j
                w = dma("sp", pool_f32(X1A[j], 2048), src_d[i * 128:(i + 1) * 128, :],
                        waits=[xfree3[j]] if xfree3[j] else [])
                vs1 = p.op("dve", lambda e, i=i, b=stats_base: e.tensor_scalar(
                    out=s_small[:, 32:33], in0=s_stats[:, b + 5, i:i + 1],
                    scalar1=s_bc[:, 0:1], scalar2=None, op0=ALU.mult),
                    waits=[w, w_bc_ready], inc=s_ve)
                vs2 = p.op("dve", lambda e, i=i, b=stats_base: e.scalar_tensor_tensor(
                    out=s_small[:, 33:34], in0=s_stats[:, b + 4, i:i + 1],
                    scalar=-1.0, in1=s_small[:, 32:33], op0=ALU.mult, op1=ALU.mult),
                    waits=[(s_ve, vs1)], inc=s_ve)
                jq = i % 2
                vq = p.op("dve", lambda e, j=j, jq=jq: e.tensor_scalar(
                    out=pool_fp8(Q8A[jq], 2048), in0=pool_f32(X1A[j], 2048),
                    scalar1=s_small[:, 32:33], scalar2=s_small[:, 33:34],
                    op0=ALU.mult, op1=ALU.add),
                    waits=[(s_ve, vs2)] + ([q8_free[jq]] if q8_free[jq] else []), inc=s_ve)
                wuq = dma("gp", UQ_d[i * 128:(i + 1) * 128, :], pool_fp8(Q8A[jq], 2048),
                          waits=[(s_ve, vq)])
                q8_free[jq] = wuq
                uq_written.append(wuq)
            ju = g % 2
            uqt_ready = None
            for kc in range(KC):
                wt = dma("sp", pool_bf16(UQT[ju] + kc * 256, 256),
                         UQ_d[g * 256:(g + 1) * 256, kc * 128:(kc + 1) * 128],
                         waits=uq_written + ([uqt_free[ju]] if (uqt_free[ju] and kc == 0) else []),
                         transpose=True)
                uqt_ready = wt
            vmm = None
            for idx, i in enumerate(tiles):
                vev = None
                for oc in range(4):
                    b = (i * 4 + oc) % 2
                    wfb = bank_free[b]
                    for kc in range(KC):
                        lhs = pool_bf16(UQT[ju] + kc * 256 + idx * 128, 128)
                        ww = ([uqt_ready, w_wq_ready] + ([wfb] if wfb else [])) if kc == 0 else []
                        p.op("pe", lambda e, lhs=lhs, kc=kc, oc=oc, b=b: e.matmul(
                            pb[b][:, 0:512], lhs, a1_t(kc)[:, oc * 512:(oc + 1) * 512],
                            start=(kc == 0), stop=False), waits=ww)
                    vmm = p.op("pe", lambda e, oc=oc, b=b: e.matmul(
                        pb[b][:, 0:512], s_onesb[:], pool_bf16(FROW + oc * 512, 512)[0:1, :],
                        start=False, stop=True), inc=s_pe)
                    bank_ready[b] = (s_pe, vmm)
                    vev = p.op("dve", lambda e, b=b, oc=oc, j=xslot[i]: e.scalar_tensor_tensor(
                        out=pool_f32(X1ST, 2048)[:, oc * 512:(oc + 1) * 512],
                        in0=pb[b][:, 0:512], scalar=s_bc[:, 1:2],
                        in1=pool_f32(X1A[j], 2048)[:, oc * 512:(oc + 1) * 512],
                        op0=ALU.mult, op1=ALU.add),
                        waits=[(s_pe, vmm)] + ([x1_free] if (x1_free and oc == 0) else []),
                        inc=s_ve)
                    bank_free[b] = (s_ve, vev)
                w_ev = (s_ve, vev)
                if out_stats_base is not None:
                    (wa2, wv2) = stats_tile(pool_f32(X1ST, 2048), i, out_stats_base,
                                            [w_ev], pool_bf16(X1A[xslot[i]], 2048),
                                            do_amax=(out_stats_base == ST3))
                    stw = [wa2] + ([wv2] if wv2 else [])
                    xfree3[xslot[i]] = wa2 if wv2 is None else wv2
                else:
                    stw = []
                    xfree3[xslot[i]] = w_ev
                wst = dma("gp", dst_d[i * 128:(i + 1) * 128, :], pool_f32(X1ST, 2048),
                          waits=[w_ev] + stw)
                w_last = wst
                if produce_u2:
                    wmr = mr_tile(out_stats_base, i, stw)
                    vu2 = p.op("dve", lambda e, i=i, b=out_stats_base: e.tensor_scalar(
                        out=pool_bf16(U2ST, 2048), in0=pool_f32(X1ST, 2048),
                        scalar1=s_stats[:, b + 5, i:i + 1],
                        scalar2=s_small[:, 34 + (i % 2):35 + (i % 2)],
                        op0=ALU.mult, op1=ALU.add),
                        waits=[wmr] + ([u2_free] if u2_free else []), inc=s_ve)
                    wu2 = dma("gp", u2_dst[i * 128:(i + 1) * 128, :], pool_bf16(U2ST, 2048),
                              waits=[(s_ve, vu2)])
                    u2_free = wu2
                    x1_free = wu2
                else:
                    x1_free = wst
            uqt_free[ju] = (s_pe, vmm)
        return w_last

    ffn1_dst = out_d if upto == 1 else X1_d
    w_x1_done = ffn_pass(x0_d, ffn1_dst, ST1, ST2, w_wq, w_sc1,
                         produce_u2=(upto >= 2), src_ready=None, u2_dst=U2_d)
    w_u2_all = (s_gp, s_gp.n)
    w_ffn1_pe = (s_pe, s_pe.n)

    def _rest():
        if upto >= 2:
            WQS = [0, 8192]
            ELU1 = 16384
            ELU3 = 18432
            QFST = [20480, 20992]
            KTM = [21504, 23552]
            VTS = [25600, 27664]
            VSTG = [16384, 16912]
            BVROW = 17440

            u2fm_last = None
            for kc in range(KC):
                u2fm_last = dma("sp", a2_t(kc), U2_d[0:T, kc * 128:(kc + 1) * 128],
                                waits=[w_u2_all], transpose=True)
            w_u2fm = u2fm_last

            elu_prev_dve = [None]

            def elu_evac(bank, bias_col, nbias_col, dst_bf16, ready, dst_war=None):
                aw = [ready] + ([elu_prev_dve[0]] if elu_prev_dve[0] else [])
                p.op("act", lambda e, b=bank, bc=bias_col: e.activation(
                    pool_f32(ELU1, 512), pb[b][:, 0:512], AF.Relu, bias=bc, scale=1.0),
                    waits=aw)
                vr2 = p.op("act", lambda e, b=bank, nbc=nbias_col: e.activation(
                    pool_f32(ELU3, 512), pb[b][:, 0:512], AF.Relu, bias=nbc, scale=-1.0),
                    inc=s_ac)
                va = p.op("act", lambda e: e.activation(
                    pool_f32(ELU3, 512), pool_f32(ELU3, 512), AF.Exp, scale=-1.0),
                    waits=[(s_ac, vr2)], inc=s_ac)
                vd = p.op("dve", lambda e, d=dst_bf16: e.tensor_tensor(
                    d, pool_f32(ELU1, 512), pool_f32(ELU3, 512), ALU.add),
                    waits=[(s_ac, va)] + ([dst_war] if dst_war else []), inc=s_ve)
                elu_prev_dve[0] = (s_ve, vd)
                return (s_ve, vd)

            w_ffn1_all = [(s_ve, s_ve.n), (s_ac, s_ac.n), (s_pe, s_pe.n), (s_gp, s_gp.n)]
            wq_free = [w_ffn1_all, w_ffn1_all]
            w_kf_all = None
            for q in range(4):
                jw = q % 2
                wwq = dma("gp", pool_bf16(WQS[jw], 8192).rearrange("p (kc n) -> p kc n", kc=KC),
                          wk_d.rearrange("(kc p) n -> p kc n", p=128)[:, :, q * 512:(q + 1) * 512],
                          waits=(wq_free[jw] if isinstance(wq_free[jw], list)
                                 else ([wq_free[jw]] if wq_free[jw] else [])))
                vmm = None
                for dl in range(4):
                    dkc = q * 4 + dl
                    for tb in range(4):
                        b = (dkc * 4 + tb) % 2
                        wfb = bank_free[b]
                        for kc in range(KC):
                            ww = ([wwq, w_u2fm] + ([wfb] if wfb else [])) if kc == 0 else []
                            last = (kc == KC - 1)
                            if last:
                                vmm = p.op("pe", lambda e, kc=kc, dl=dl, tb=tb, b=b, jw=jw: e.matmul(
                                    pb[b][:, 0:512],
                                    pool_bf16(WQS[jw] + kc * 512 + dl * 128, 128),
                                    a2_t(kc)[:, tb * 512:(tb + 1) * 512],
                                    start=False, stop=True), waits=ww, inc=s_pe)
                            else:
                                p.op("pe", lambda e, kc=kc, dl=dl, tb=tb, b=b, jw=jw: e.matmul(
                                    pb[b][:, 0:512],
                                    pool_bf16(WQS[jw] + kc * 512 + dl * 128, 128),
                                    a2_t(kc)[:, tb * 512:(tb + 1) * 512],
                                    start=(kc == 0), stop=False), waits=ww)
                        bank_ready[b] = (s_pe, vmm)
                        wkf = elu_evac(b, s_cols[:, 2, dkc:dkc + 1], s_cols[:, 3, dkc:dkc + 1],
                                       a1_t(dkc)[:, tb * 512:(tb + 1) * 512], (s_pe, vmm),
                                       dst_war=w_ffn1_pe)
                        bank_free[b] = wkf
                    wkfs = dma("gp", KF_d[dkc * 128:(dkc + 1) * 128, :], a1_t(dkc),
                               waits=[bank_free[0], bank_free[1]])
                    w_kf_all = wkfs
                wq_free[jw] = (s_pe, vmm)

            if astage < 2:
                dma("gp", out_d, X1_d, waits=[w_x1_done, (s_ve, s_ve.n), (s_ac, s_ac.n), (s_pe, s_pe.n), (s_gp, s_gp.n), (s_ld, s_ld.n)])
                raise _StageStop
            vt_free = [None, None]
            vt_stored = [None] * NT
            vst_ready = [None, None]
            w_passA_act = (s_ac, s_ac.n)
            w_bvrow = dma("gp", pool_bf16(BVROW, 2048)[0:1, :], bv_d,
                          waits=[(s_pe, s_pe.n)])
            for j in range(2):
                vv = p.op("dve", lambda e, j=j: e.memset(pool_bf16(VSTG[j], 516), 1.0),
                          waits=[w_ffn1_pe, w_passA_act], inc=s_ve)
                vst_ready[j] = (s_ve, vv)
            for q in range(4):
                jw = q % 2
                wwq = dma("gp", pool_bf16(WQS[jw], 8192).rearrange("p (kc n) -> p kc n", kc=KC),
                          wv_d.rearrange("(kc p) n -> p kc n", p=128)[:, :, q * 512:(q + 1) * 512],
                          waits=[wq_free[jw]] if wq_free[jw] else [])
                vmm = None
                for tt in range(NT):
                    b = (q * NT + tt) % 2
                    wfb = bank_free[b]
                    for kc in range(KC):
                        ww = ([wwq, w_u2fm] + ([wfb] if wfb else [])) if kc == 0 else []
                        p.op("pe", lambda e, kc=kc, tt=tt, b=b, jw=jw: e.matmul(
                            pb[b][:, 0:512],
                            a2_t(kc)[:, tt * 128:(tt + 1) * 128],
                            pool_bf16(WQS[jw] + kc * 512, 512),
                            start=(kc == 0), stop=False), waits=ww)
                    vmm = p.op("pe", lambda e, b=b, q=q: e.matmul(
                        pb[b][:, 0:512], s_onesb[:], pool_bf16(BVROW + q * 512, 512)[0:1, :],
                        start=False, stop=True), waits=[w_bvrow], inc=s_pe)
                    bank_ready[b] = (s_pe, vmm)
                    jv = tt % 2
                    vv = None
                    for hh in range(4):
                        vv = p.op("dve", lambda e, b=b, hh=hh, jv=jv: e.tensor_copy(
                            pool_bf16(VSTG[jv] + hh * 129, 128),
                            pb[b][:, hh * 128:(hh + 1) * 128]),
                            waits=[(s_pe, vmm), vst_ready[jv]] +
                                  ([vt_free[jv]] if (vt_free[jv] and hh == 0) else []),
                            inc=(s_ve if hh == 3 else None))
                    bank_free[b] = (s_ve, vv)
                    wvt = dma("gp", VT_d[tt, q, :, :], pool_bf16(VSTG[jv], 516),
                              waits=[(s_ve, vv)])
                    vt_free[jv] = wvt
                    vt_stored[tt] = wvt
                wq_free[jw] = (s_pe, vmm)
            w_vt_all = (s_gp, s_gp.n)

            if astage < 3:
                dma("gp", out_d, X1_d, waits=[w_x1_done, (s_ve, s_ve.n), (s_ac, s_ac.n), (s_pe, s_pe.n), (s_gp, s_gp.n), (s_ld, s_ld.n)])
                raise _StageStop
            qf_free = [None, None]
            ktm_free = [None, None]
            vtb_free = [None, None]
            kv_last = {}
            ks_last = None
            for q in range(4):
                jw = q % 2
                wwq = dma("gp", pool_bf16(WQS[jw], 8192).rearrange("p (kc n) -> p kc n", kc=KC),
                          wq_d.rearrange("(kc p) n -> p kc n", p=128)[:, :, q * 512:(q + 1) * 512],
                          waits=[wq_free[jw]] if wq_free[jw] else [])
                vmm = None
                for dl in range(4):
                    dqc = q * 4 + dl
                    for tb in range(4):
                        b = (dqc * 4 + tb) % 2
                        wfb = bank_free[b]
                        for kc in range(KC):
                            ww = ([wwq, w_u2fm] + ([wfb] if wfb else [])) if kc == 0 else []
                            last = (kc == KC - 1)
                            if last:
                                vmm = p.op("pe", lambda e, kc=kc, dl=dl, tb=tb, b=b, jw=jw: e.matmul(
                                    pb[b][:, 0:512],
                                    pool_bf16(WQS[jw] + kc * 512 + dl * 128, 128),
                                    a2_t(kc)[:, tb * 512:(tb + 1) * 512],
                                    start=False, stop=True), waits=ww, inc=s_pe)
                            else:
                                p.op("pe", lambda e, kc=kc, dl=dl, tb=tb, b=b, jw=jw: e.matmul(
                                    pb[b][:, 0:512],
                                    pool_bf16(WQS[jw] + kc * 512 + dl * 128, 128),
                                    a2_t(kc)[:, tb * 512:(tb + 1) * 512],
                                    start=(kc == 0), stop=False), waits=ww)
                        bank_ready[b] = (s_pe, vmm)
                        jf = (dqc * 4 + tb) % 2
                        wqf = elu_evac(b, s_cols[:, 0, dqc:dqc + 1], s_cols[:, 1, dqc:dqc + 1],
                                       pool_bf16(QFST[jf], 512), (s_pe, vmm),
                                       dst_war=(qf_free[jf] if qf_free[jf] else None))
                        bank_free[b] = wqf
                        wqs = dma("gp", QF_d[dqc * 128:(dqc + 1) * 128, tb * 512:(tb + 1) * 512],
                                  pool_bf16(QFST[jf], 512), waits=[wqf])
                        qf_free[jf] = wqs
                wq_free[jw] = (s_pe, vmm)
                for tt in (range(q * 4, q * 4 + 4) if astage != 31 else []):
                    jk = tt % 2
                    wk1 = dma("sp", pool_bf16(KTM[jk], 2048), KF_d[0:T, tt * 128:(tt + 1) * 128],
                              waits=[w_kf_all] + ([ktm_free[jk]] if ktm_free[jk] else []),
                              transpose=True)
                    wv1 = dma("gp", pool_bf16(VTS[jk], 2064).rearrange("p (qq n) -> p qq n", qq=4),
                              VT_d[tt, :, :, :].rearrange("qq p e -> p qq e"),
                              waits=[vt_stored[tt]] + ([vtb_free[jk]] if vtb_free[jk] else []))
                    for h in range(H):
                        bb = 2 + h // 4
                        wfb = bank_free[bb] if (tt == 0 and h % 4 == 0) else None
                        vkv = p.op("pe", lambda e, bb=bb, h=h, jk=jk, tt=tt: e.matmul(
                            pb[bb][:, (h % 4) * 128:(h % 4) * 128 + 128],
                            pool_bf16(KTM[jk] + h * 128, 128),
                            pool_bf16(VTS[jk] + (h // 4) * 516 + (h % 4) * 129, 128),
                            start=(tt == 0 and h % 4 == 0),
                            stop=(tt == NT - 1 and h % 4 == 3), skip_group_check=True),
                            waits=[wk1, wv1] + ([wfb] if wfb else []),
                            inc=(s_pe if tt == NT - 1 else None))
                        wfb6 = bank_free[6] if (tt == 0 and h == 0) else None
                        vks = p.op("pe", lambda e, h=h, jk=jk, tt=tt: e.matmul(
                            pb[6][:, h:h + 1],
                            pool_bf16(KTM[jk] + h * 128, 128),
                            s_onescol[:],
                            start=(tt == 0 and h == 0),
                            stop=(tt == NT - 1 and h == H - 1), skip_group_check=True),
                            waits=([wfb6] if wfb6 else []) + [w_const],
                            inc=(s_pe if (tt == NT - 1 or h == H - 1) else None))
                        if tt == NT - 1:
                            kv_last[h] = (s_pe, vkv)
                            ks_last = (s_pe, vks)
                        if h == H - 1:
                            ktm_free[jk] = (s_pe, vks)
                            vtb_free[jk] = (s_pe, vks)
            w_qf_all = (s_gp, s_gp.n)
            if astage in (31, 32):
                dma("gp", out_d, X1_d, waits=[w_x1_done, (s_ve, s_ve.n), (s_ac, s_ac.n), (s_pe, s_pe.n), (s_gp, s_gp.n), (s_ld, s_ld.n)])
                raise _StageStop

            vst = None
            for h in range(H):
                bb = 2 + h // 4
                vst = p.op("dve", lambda e, bb=bb, h=h: e.tensor_copy(
                    s_st[:, h, 0:128], pb[bb][:, (h % 4) * 128:(h % 4) * 128 + 128]),
                    waits=[kv_last[(h // 4) * 4 + 3]], inc=s_ve)
                bank_free[bb] = (s_ve, vst)
            vst = p.op("dve", lambda e: e.tensor_copy(
                s_st[:, :, 128], pb[6][:, 0:16]), waits=[ks_last], inc=s_ve)
            bank_free[6] = (s_ve, vst)
            w_tot = (s_ve, vst)

            if astage == 33:
                dma("gp", out_d, X1_d, waits=[w_x1_done, w_tot, (s_ve, s_ve.n), (s_ac, s_ac.n), (s_pe, s_pe.n), (s_gp, s_gp.n), (s_ld, s_ld.n)])
                raise _StageStop
            vcc = p.op("dve", lambda e: e.tensor_scalar(
                out=s_st[:].rearrange("p h e -> p (h e)"),
                in0=s_st[:].rearrange("p h e -> p (h e)"),
                scalar1=s_flag[:, 1:2], scalar2=None, op0=ALU.mult),
                waits=[w_tot], inc=s_ve)
            wcc1 = dma("gp", cc_kv_in.ap().rearrange("(h p) e -> p h e", p=128), s_st[:],
                       waits=[(s_ve, vcc)])
            p.op("po", lambda e: e.collective_compute(
                "AllReduce", ALU.add, replica_groups=[[0, 1], [2, 3], [4, 5], [6, 7]],
                ins=[cc_kv_in.ap().opt()], outs=[cc_kv_out.ap().opt()]),
                waits=[wcc1], inc=s_c2)
            wcc2 = dma("gp", s_st[:], cc_kv_out.ap().rearrange("(h p) e -> p h e", p=128),
                       waits=[(s_c2, 1)])
            vst0 = p.op("dve", lambda e: e.tensor_scalar(
                out=s_st[:].rearrange("p h e -> p (h e)"),
                in0=s_st[:].rearrange("p h e -> p (h e)"),
                scalar1=s_flag[:, 0:1], scalar2=None, op0=ALU.mult),
                waits=[wcc2], inc=s_ve)
            w_st_init = (s_ve, vst0)

            if astage < 4:
                dma("gp", out_d, X1_d, waits=[w_x1_done, w_st_init, (s_ve, s_ve.n), (s_ac, s_ac.n), (s_pe, s_pe.n), (s_gp, s_gp.n), (s_ld, s_ld.n)])
                raise _StageStop
            w_b_pe = (s_pe, s_pe.n)
            wo_ready = None
            for kc in range(KC):
                wo_ready = dma("gp", a2_t(kc), wo_d[kc * 128:(kc + 1) * 128, :],
                               waits=[w_b_pe])
            w_wo = wo_ready

            QFC = [0, 2048]
            KTC = [4096, 6144]
            VTC = [8192, 10256]
            X1C = [12320, 16416]
            X2ST = 20512
            SMO = 24608
            OFM = 26656
            STB = 28704

            qfc_free = [None, None]
            ktc_free = [None, None]
            vtc_free = [None, None]
            x1c_free = [None, None]
            x2_free = None
            ofm_free = None
            prev_updates = [w_st_init]
            SGRP = [[0, 1, 2, 3], [4, 5, 6, 7], [8, 9, 10, 11], [12, 13, 14, 15]]
            KVGRP = [[0, 1, 2], [3, 4, 5], [6, 7, 8], [9, 10, 11], [12, 13, 14], [15]]
            OGRP = [(4, [0, 1, 2]), (5, [3, 4, 5]), (4, [6, 7, 8]), (5, [9, 10, 11]),
                    (4, [12, 13, 14]), (5, [15])]

            chunk_w = {}

            def load_chunk(n):
                jq = n % 2
                wq_ = dma("gp", pool_bf16(QFC[jq], 2048).rearrange("p (kc t) -> p kc t", kc=KC),
                          QF_d.rearrange("(kc p) t -> p kc t", p=128)[:, :, n * 128:(n + 1) * 128],
                          waits=[w_qf_all] + ([qfc_free[jq]] if qfc_free[jq] else []))
                wk_ = dma("sp", pool_bf16(KTC[jq], 2048), KF_d[0:T, n * 128:(n + 1) * 128],
                          waits=[w_kf_all] + ([ktc_free[jq]] if ktc_free[jq] else []),
                          transpose=True)
                wv_ = dma("gp", pool_bf16(VTC[jq], 2064).rearrange("p (qq nn) -> p qq nn", qq=4),
                          VT_d[n, :, :, :].rearrange("qq pp e -> pp qq e"),
                          waits=[w_vt_all] + ([vtc_free[jq]] if vtc_free[jq] else []))
                wx_ = dma("sp", pool_f32(X1C[jq], 2048), X1_d[n * 128:(n + 1) * 128, :],
                          waits=[w_x1_done] + ([x1c_free[jq]] if x1c_free[jq] else []))
                chunk_w[n] = (wq_, wk_, wv_, wx_)

            load_chunk(0)
            load_chunk(1)
            dst2_d = out_d if upto == 2 else X2_d
            w_x2_done = None
            for n in range(NT):
                jq = n % 2
                wq_, wk_, wv_, wx_ = chunk_w[n]
                va = p.op("act", lambda e: e.activation(
                    pool_bf16(STB, 2064),
                    s_st[:].rearrange("p h e -> p (h e)"),
                    AF.Identity, scale=1.0),
                    waits=prev_updates + [w_st_init], inc=s_ac)
                w_stb = (s_ac, va)
                prev_updates = []
                sm_ready = {}
                for gi, heads in enumerate(SGRP):
                    b = 2 + gi % 2
                    wfb = bank_free[b]
                    vmm = None
                    for ih, h in enumerate(heads):
                        vmm = p.op("pe", lambda e, b=b, ih=ih, h=h, jq=jq: e.matmul(
                            pb[b][:, ih * 128:(ih + 1) * 128],
                            pool_bf16(KTC[jq] + h * 128, 128),
                            pool_bf16(QFC[jq] + h * 128, 128),
                            start=(ih == 0), stop=(ih == 3), skip_group_check=True),
                            waits=([wk_, wq_] + ([wfb] if wfb else [])) if ih == 0 else [],
                            inc=(s_pe if ih == 3 else None))
                    bank_ready[b] = (s_pe, vmm)
                    vm = p.op("dve", lambda e, b=b, g=gi: e.tensor_tensor(
                        pool_bf16(SMO + g * 512, 512), pb[b][:, 0:512], s_mask[:], ALU.mult),
                        waits=[(s_pe, vmm)] + ([ofm_free] if (ofm_free and gi == 0) else []),
                        inc=s_ve)
                    bank_free[b] = (s_ve, vm)
                    for h in heads:
                        sm_ready[h] = (s_ve, vm)
                for gi, heads in enumerate(KVGRP):
                    b = 2 + gi % 2
                    wfb = bank_free[b]
                    vmm = None
                    for ih, h in enumerate(heads):
                        vmm = p.op("pe", lambda e, b=b, ih=ih, h=h, jq=jq, la=(h == heads[-1]): e.matmul(
                            pb[b][:, ih * 129:(ih + 1) * 129],
                            pool_bf16(KTC[jq] + h * 128, 128),
                            pool_bf16(VTC[jq] + (h // 4) * 516 + (h % 4) * 129, 129),
                            start=(ih == 0), stop=la, skip_group_check=True),
                            waits=([wv_, wk_] + ([wfb] if wfb else [])) if ih == 0 else [],
                            inc=(s_pe if h == heads[-1] else None))
                    bank_ready[b] = (s_pe, vmm)
                    nh = len(heads)
                    h0 = heads[0]
                    vu = p.op("dve", lambda e, b=b, h0=h0, nh=nh: e.tensor_tensor(
                        s_st[:, h0:h0 + nh, :].rearrange("p h e -> p (h e)"),
                        s_st[:, h0:h0 + nh, :].rearrange("p h e -> p (h e)"),
                        pb[b][:, 0:nh * 129], ALU.add),
                        waits=[(s_pe, vmm), w_stb], inc=s_ve)
                    bank_free[b] = (s_ve, vu)
                    prev_updates.append((s_ve, vu))
                o_scaled = {}
                for gi, (b, heads) in enumerate(OGRP):
                    wfb = bank_free[b]
                    vmm = None
                    for ih, h in enumerate(heads):
                        p.op("pe", lambda e, b=b, ih=ih, h=h, jq=jq: e.matmul(
                            pb[b][:, ih * 129:(ih + 1) * 129],
                            pool_bf16(QFC[jq] + h * 128, 128),
                            pool_bf16(STB + h * 129, 129),
                            start=(ih == 0), stop=False, skip_group_check=True),
                            waits=([w_stb] + ([wfb] if wfb else [])) if ih == 0 else [])
                        vmm = p.op("pe", lambda e, b=b, ih=ih, h=h, jq=jq, la=(h == heads[-1]): e.matmul(
                            pb[b][:, ih * 129:(ih + 1) * 129],
                            pool_bf16(SMO + h * 128, 128),
                            pool_bf16(VTC[jq] + (h // 4) * 516 + (h % 4) * 129, 129),
                            start=False, stop=la, skip_group_check=True),
                            waits=[sm_ready[h], wv_],
                            inc=(s_pe if h == heads[-1] else None))
                    bank_ready[b] = (s_pe, vmm)
                    nh = len(heads)
                    vze = None
                    for ih in range(nh):
                        vze = p.op("dve", lambda e, b=b, ih=ih: e.tensor_scalar(
                            out=s_small[:, 20 + ih:21 + ih],
                            in0=pb[b][:, ih * 129 + 128:ih * 129 + 129],
                            scalar1=EPS_Z, scalar2=None, op0=ALU.add),
                            waits=[(s_pe, vmm)] if ih == 0 else [],
                            inc=(s_ve if ih == nh - 1 else None))
                    vrz = p.op("dve", lambda e, nh=nh: e.reciprocal(
                        s_small[:, 24:24 + nh], s_small[:, 20:20 + nh]),
                        waits=[(s_ve, vze)], inc=s_ve)
                    vsc = None
                    for ih, h in enumerate(heads):
                        vsc = p.op("dve", lambda e, b=b, ih=ih, h=h: e.tensor_scalar(
                            out=pool_bf16(SMO + h * 128, 128),
                            in0=pb[b][:, ih * 129:ih * 129 + 128],
                            scalar1=s_small[:, 24 + ih:25 + ih], scalar2=None, op0=ALU.mult),
                            waits=[(s_ve, vrz), sm_ready[h]],
                            inc=(s_ve if h == heads[-1] else None))
                    bank_free[b] = (s_ve, vsc)
                    for h in heads:
                        o_scaled[h] = (s_ve, vsc)
                ofm_ready = []
                vof = None
                for tg in range(4):
                    wfb7 = bank_free[7]
                    vtp = None
                    for ih in range(4):
                        h = tg * 4 + ih
                        vtp = p.op("pe", lambda e, ih=ih, h=h: e.transpose(
                            pb7[:, ih * 128:(ih + 1) * 128],
                            pool_bf16(SMO + h * 128, 128), s_ident[:]),
                            waits=[o_scaled[h]] + ([wfb7] if (wfb7 and ih == 0) else []),
                            inc=(s_pe if ih == 3 else None))
                    bank_ready[7] = (s_pe, vtp)
                    vof = p.op("act", lambda e, tg=tg: e.copy(
                        pool_bf16(OFM + tg * 512, 512), pb7[:, 0:512]),
                        waits=[(s_pe, vtp)] + ([ofm_free] if (ofm_free and tg == 0) else []),
                        inc=s_ac)
                    bank_free[7] = (s_ac, vof)
                    ofm_ready.append((s_ac, vof))
                vx2 = None
                for oc in range(4):
                    b = (n * 4 + oc) % 2
                    wfb = bank_free[b]
                    for kc in range(KC):
                        ww = ([ofm_ready[3], w_wo] + ([wfb] if wfb else [])) if kc == 0 else []
                        p.op("pe", lambda e, kc=kc, oc=oc, b=b: e.matmul(
                            pb[b][:, 0:512], pool_bf16(OFM + kc * 128, 128),
                            a2_t(kc)[:, oc * 512:(oc + 1) * 512],
                            start=(kc == 0), stop=False), waits=ww)
                    vmm = p.op("pe", lambda e, oc=oc, b=b: e.matmul(
                        pb[b][:, 0:512], s_onesb[:], s_rowb[0:1, 0, oc * 512:(oc + 1) * 512],
                        start=False, stop=True), inc=s_pe)
                    bank_ready[b] = (s_pe, vmm)
                    vx2 = p.op("dve", lambda e, b=b, oc=oc, jq=jq: e.scalar_tensor_tensor(
                        out=pool_f32(X2ST, 2048)[:, oc * 512:(oc + 1) * 512],
                        in0=pb[b][:, 0:512], scalar=1.0,
                        in1=pool_f32(X1C[jq], 2048)[:, oc * 512:(oc + 1) * 512],
                        op0=ALU.mult, op1=ALU.add),
                        waits=[(s_pe, vmm), wx_] + ([x2_free] if (x2_free and oc == 0) else []),
                        inc=s_ve)
                    bank_free[b] = (s_ve, vx2)
                w_x2t = (s_ve, vx2)
                x1c_free[jq] = w_x2t
                (wa3, wv3) = stats_tile(pool_f32(X2ST, 2048), n, ST3, [w_x2t],
                                        pool_bf16(OFM, 2048), do_amax=True)
                ofm_free = wa3
                wst2 = dma("gp", dst2_d[n * 128:(n + 1) * 128, :], pool_f32(X2ST, 2048),
                           waits=[w_x2t, wa3] + ([wv3] if wv3 else []))
                x2_free = wst2
                wmr3 = mr_tile(ST3, n, [wa3] + ([wv3] if wv3 else []))
                amax_cols(ST3, slice(n, n + 1), [wmr3, wv3])
                qfc_free[jq] = (s_pe, s_pe.n)
                ktc_free[jq] = (s_pe, s_pe.n)
                vtc_free[jq] = (s_pe, s_pe.n)
                if n + 2 < NT:
                    load_chunk(n + 2)
                w_x2_done = wst2

        if upto >= 3:
            w_am3 = (s_ve, s_ve.n)
            w_2d_gp = (s_gp, s_gp.n)
            w_2d_pe = (s_pe, s_pe.n)
            w_2d_ve = (s_ve, s_ve.n)
            row_scr3 = pool_f32(ROWS, 2048)
            w_ax3 = scalarize(s_stats[:, ST3 + 6, :], NT, 2, s_sc[0:1, 0:1], [w_am3], row_scr3)
            w13 = dma("gp", cc_am_in2.ap()[0:1, 0:1], s_sc[0:1, 0:1], waits=[w_ax3])
            p.op("po", lambda e: e.collective_compute(
                "AllReduce", ALU.max, replica_groups=[list(range(NCORE))],
                ins=[cc_am_in2.ap().opt()], outs=[cc_am_out2.ap().opt()]),
                waits=[w13], inc=s_c3)
            w23 = dma("gp", s_sc[0:1, 0:1], cc_am_out2.ap()[0:1, 0:1], waits=[(s_c3, 1)])
            w_sc3 = ffn_scales([w23], [w_2d_ve, w_2d_pe])
            w_wq3 = w_quant_pass(w_sc3, w_2d_ve, w_2d_pe)
            ffn_pass(X2_d, out_d, ST3, None, w_wq3, w_sc3, produce_u2=False,
                     src_ready=w_x2_done)

        return
    try:
        _rest()
    except _StageStopCls:
        pass
    if dbg:
        o_stats = nc.dram_tensor("o_stats", [128, 24 * KC], F32, kind="ExternalOutput").ap()
        o_sc = nc.dram_tensor("o_sc", [1, 32], F32, kind="ExternalOutput").ap()
        o_bc = nc.dram_tensor("o_bc", [128, 8], F32, kind="ExternalOutput").ap()
        o_uq = nc.dram_tensor("o_uq", [T, D], BF16, kind="ExternalOutput").ap()
        o_x1 = nc.dram_tensor("o_x1", [T, D], F32, kind="ExternalOutput").ap()
        o_u2 = nc.dram_tensor("o_u2", [T, D], BF16, kind="ExternalOutput").ap()
        o_a1 = nc.dram_tensor("o_a1", [128, KC * D], BF16, kind="ExternalOutput").ap()
        wend = [(s_gp, s_gp.n), (s_ld, s_ld.n), (s_ve, s_ve.n), (s_ac, s_ac.n), (s_pe, s_pe.n)]
        dma("gp", o_stats, s_stats[:].rearrange("p a b -> p (a b)"), waits=wend)
        dma("gp", o_sc, s_sc[:], waits=[])
        dma("gp", o_bc, s_bc[:], waits=[])
        dma("gp", o_uq, UQ_d, waits=[])
        dma("gp", o_x1, X1_d, waits=[])
        dma("gp", o_u2, U2_d, waits=[])
        dma("gp", o_a1, A1[:].rearrange("p a b -> p (a b)"), waits=[])
    p.op("po", lambda e: None, waits=[(s_gp, s_gp.n), (s_ld, s_ld.n)])

    with es:
        with nc.Block() as block:
            def emitter(name):
                def f(eng):
                    for waits, fn, inc in p.ops[name]:
                        for h, v in waits:
                            eng.wait_ge(h, v)
                        inst = fn(eng)
                        if inc is not None:
                            if inst is None:
                                inst = eng.nop()
                            inst.then_inc(inc[0], inc[1])
                return f
            block.sync(emitter("sp"))
            block.scalar(emitter("act"))
            block.vector(emitter("dve"))
            block.tensor(emitter("pe"))
            block.gpsimd(emitter("po"))
    return nc


def _shard_inputs(L0_attn_x, L0_attn_Wq, L0_attn_bq, L0_attn_Wk, L0_attn_bk,
                  L0_attn_Wv, L0_attn_bv, L0_attn_Wo, L0_attn_bo,
                  L0_ffn_weight, L0_ffn_bias):
    import ml_dtypes
    xs = np.ascontiguousarray(np.asarray(L0_attn_x, dtype=np.float32)).reshape(16384, 2048)
    shared = dict(
        wq=np.ascontiguousarray(np.asarray(L0_attn_Wq, np.float32)),
        wk=np.ascontiguousarray(np.asarray(L0_attn_Wk, np.float32)),
        wv=np.ascontiguousarray(np.asarray(L0_attn_Wv, np.float32)),
        wo=np.ascontiguousarray(np.asarray(L0_attn_Wo, np.float32)),
        wt=np.ascontiguousarray(np.asarray(L0_ffn_weight, np.float32).T),
        bq_cols=np.ascontiguousarray(np.asarray(L0_attn_bq, np.float32).reshape(16, 128).T),
        bk_cols=np.ascontiguousarray(np.asarray(L0_attn_bk, np.float32).reshape(16, 128).T),
        bv_row=np.asarray(L0_attn_bv, np.float32).reshape(1, 2048),
        bo_row=np.asarray(L0_attn_bo, np.float32).reshape(1, 2048),
        bf_row=np.asarray(L0_ffn_bias, np.float32).reshape(1, 2048),
        ident=np.eye(128).astype(ml_dtypes.bfloat16),
        ones_row=np.ones((1, 128), np.float32),
    )
    m = (np.arange(128)[:, None] <= np.arange(128)[None, :]).astype(np.float32)
    shared["mask4"] = np.tile(m, (1, 4)).astype(ml_dtypes.bfloat16)
    in_maps = []
    for c in range(NCORE):
        d = dict(shared)
        d["x0"] = np.ascontiguousarray(xs[c * 2048:(c + 1) * 2048])
        d["oddflag"] = np.full((128, 1), float(c % 2), np.float32)
        d["evenflag"] = np.full((128, 1), float(1 - c % 2), np.float32)
        in_maps.append(d)
    return in_maps


def kernel(L0_attn_x, L0_attn_Wq, L0_attn_bq, L0_attn_Wk, L0_attn_bk,
           L0_attn_Wv, L0_attn_bv, L0_attn_Wo, L0_attn_bo,
           L0_ffn_weight, L0_ffn_bias):
    in_maps = _shard_inputs(L0_attn_x, L0_attn_Wq, L0_attn_bq, L0_attn_Wk,
                            L0_attn_bk, L0_attn_Wv, L0_attn_bv, L0_attn_Wo,
                            L0_attn_bo, L0_ffn_weight, L0_ffn_bias)
    nc = build_program(upto=3)
    res = run_bass_kernel_spmd(nc, in_maps, core_ids=list(range(NCORE)))
    out = np.concatenate([r["out"] for r in res.results], axis=0)
    return out.reshape(4, 4096, 2048)


# version 02 (final)
# speedup vs baseline: 1.0413x; 1.0413x over previous
import contextlib
import os
import numpy as np

import concourse.bass as bass
import concourse.mybir as mybir
from concourse.bass_utils import run_bass_kernel_spmd

F32 = mybir.dt.float32
BF16 = mybir.dt.bfloat16
FP8 = mybir.dt.float8e4
AF = mybir.ActivationFunctionType
ALU = mybir.AluOpType
AX = mybir.AxisListType

NCORE = 8
T = 2048
D = 2048
NT = 16
KC = 16
H = 16
EPS_LN = 1e-5
EPS_Z = 1e-6
FP8_HALF_SCALE = 179.2
DEQ_CONST = 2.0 / (358.4 * 358.4)

ENGS = ("sp", "act", "dve", "pe", "po")

ST1 = 0
STW = 7
ST2 = 8
ST3 = 16


class Sem:
    def __init__(self, handle, name):
        self.h = handle
        self.name = name
        self.n = 0


class Prog:
    def __init__(self):
        self.ops = {e: [] for e in ENGS}
        self.waited = {e: {} for e in ENGS}

    def op(self, eng, fn, waits=(), inc=None, amt=1):
        real = []
        for wv in waits:
            if wv is None:
                continue
            s, v = wv
            if v is None:
                continue
            w = self.waited[eng]
            if w.get(s.name, -1) >= v:
                continue
            w[s.name] = v
            real.append((s.h, v))
        post = None
        if inc is not None:
            inc.n += amt
            post = inc.n
        self.ops[eng].append((real, fn, (inc.h, amt) if inc else None))
        return post


def build_program(upto=3, dbg=False, astage=99):
    nc = bass.Bass("TRN2", target_bir_lowering=False)

    x0_d = nc.dram_tensor("x0", [T, D], F32, kind="ExternalInput").ap()
    wq_d = nc.dram_tensor("wq", [D, D], F32, kind="ExternalInput").ap()
    wk_d = nc.dram_tensor("wk", [D, D], F32, kind="ExternalInput").ap()
    wv_d = nc.dram_tensor("wv", [D, D], F32, kind="ExternalInput").ap()
    wo_d = nc.dram_tensor("wo", [D, D], F32, kind="ExternalInput").ap()
    wt_d = nc.dram_tensor("wt", [D, D], F32, kind="ExternalInput").ap()
    bq_d = nc.dram_tensor("bq_cols", [128, KC], F32, kind="ExternalInput").ap()
    bk_d = nc.dram_tensor("bk_cols", [128, KC], F32, kind="ExternalInput").ap()
    bv_d = nc.dram_tensor("bv_row", [1, D], F32, kind="ExternalInput").ap()
    bo_d = nc.dram_tensor("bo_row", [1, D], F32, kind="ExternalInput").ap()
    bf_d = nc.dram_tensor("bf_row", [1, D], F32, kind="ExternalInput").ap()
    mask_d = nc.dram_tensor("mask4", [128, 512], BF16, kind="ExternalInput").ap()
    ident_d = nc.dram_tensor("ident", [128, 128], BF16, kind="ExternalInput").ap()
    ones_d = nc.dram_tensor("ones_row", [1, 128], F32, kind="ExternalInput").ap()
    oddf_d = nc.dram_tensor("oddflag", [128, 1], F32, kind="ExternalInput").ap()
    evenf_d = nc.dram_tensor("evenflag", [128, 1], F32, kind="ExternalInput").ap()
    out_d = nc.dram_tensor("out", [T, D], F32, kind="ExternalOutput").ap()

    X1_d = nc.dram_tensor("X1s", [T, D], F32).ap()
    X2_d = nc.dram_tensor("X2s", [T, D], F32).ap()
    U2_d = nc.dram_tensor("U2s", [T, D], BF16).ap()
    UQ_d = nc.dram_tensor("UQs", [T, D], BF16).ap()
    QF_d = nc.dram_tensor("QFs", [D, T], BF16).ap()
    KF_d = nc.dram_tensor("KFs", [D, T], BF16).ap()
    VT_d = nc.dram_tensor("VTs", [NT, 4, 128, 516], BF16).ap()
    SCR_ds = [nc.dram_tensor("SCRs%d" % i, [128, 16], F32).ap() for i in range(3)]
    cc_am_in = nc.dram_tensor("cc_am_in", [1, 8], F32)
    cc_am_out = nc.dram_tensor("cc_am_out", [1, 8], F32)
    cc_am_in2 = nc.dram_tensor("cc_am_in2", [1, 8], F32)
    cc_am_out2 = nc.dram_tensor("cc_am_out2", [1, 8], F32)
    cc_kv_in = nc.dram_tensor("cc_kv_in", [H * 128, 129], F32)
    cc_kv_out = nc.dram_tensor("cc_kv_out", [H * 128, 129], F32)

    class _StageStopCls(Exception):
        pass
    _StageStop = _StageStopCls()

    p = Prog()
    es = contextlib.ExitStack()
    SB = lambda name, shape, dt: es.enter_context(nc.sbuf_tensor(name, shape, dt))

    A1 = SB("A1", [128, KC, D], BF16)
    A2 = SB("A2", [128, KC, D], BF16)
    POOL = SB("POOL", [128, 30768], BF16)
    s_st = SB("s_st", [128, H, 129], F32)
    s_stats = SB("s_stats", [128, 24, KC], F32)
    s_mask = SB("s_mask", [128, 512], BF16)
    s_ident = SB("s_ident", [128, 128], BF16)
    s_cols = SB("s_cols", [128, 4, KC], F32)
    s_bc = SB("s_bc", [128, 8], F32)
    s_flag = SB("s_flag", [128, 2], F32)
    s_rowb = SB("s_rowb", [1, 1, D], BF16)
    s_ones = SB("s_ones", [1, 128], F32)
    s_onesb = SB("s_onesb", [1, 128], BF16)
    s_onescol = SB("s_onescol", [128, 1], BF16)
    s_ones8 = SB("s_ones8", [128, 32], BF16)
    s_sc = SB("s_sc", [1, 32], F32)
    s_small = SB("s_small", [128, 40], F32)

    pb = [es.enter_context(nc.psum_tensor("pb%d" % i, [128, 512], F32)) for i in range(7)]
    pb7 = es.enter_context(nc.psum_tensor("pb7", [128, 1024], BF16))

    sems = {}
    for name in ("ld", "gp", "pe", "ve", "ac", "po", "c1", "c2", "c3"):
        sems[name] = Sem(es.enter_context(nc.semaphore(name)), name)
    s_ld, s_gp, s_pe, s_ve, s_ac = (sems[k] for k in ("ld", "gp", "pe", "ve", "ac"))
    s_c1, s_c2, s_c3 = sems["c1"], sems["c2"], sems["c3"]

    bank_free = [None] * 8
    bank_ready = [None] * 8

    PF = POOL[:].bitcast(F32)
    FPv = POOL[:].bitcast(FP8)

    def pool_f32(off, n):
        return PF[:, off // 2: off // 2 + n]

    def pool_bf16(off, n):
        return POOL[:, off: off + n]

    def pool_fp8(off, n):
        return FPv[:, off * 2: off * 2 + n]

    def a1_t(kc):
        return A1[:, kc, :]

    def a2_t(kc):
        return A2[:, kc, :]

    def dma(eng, out_ap, in_ap, waits=(), transpose=False):
        s = s_ld if eng == "sp" else s_gp
        eng = "sp" if eng == "sp" else "po"
        if transpose:
            fn = (lambda e, o=out_ap, i=in_ap: e.dma_start_transpose(o, i))
        else:
            fn = (lambda e, o=out_ap, i=in_ap: e.dma_start(out=o, in_=i))
        v = p.op(eng, fn, waits=waits, inc=s, amt=16)
        return (s, v)

    def scalarize(src_ap, ncols, scr_idx, dst_ap, waits, row_scratch):
        scr = SCR_ds[scr_idx]
        w1 = dma("gp", scr[:, 0:ncols], src_ap, waits=waits)
        w2 = dma("gp", row_scratch[0:1, 0:128 * ncols],
                 scr.rearrange("p n -> (p n)")[None, :],
                 waits=[w1])
        v = p.op("dve", lambda e, d=dst_ap, s=row_scratch, n=128 * ncols:
                 e.tensor_reduce(d, s[0:1, 0:n], AX.X, ALU.max),
                 waits=[w2], inc=s_ve)
        return (s_ve, v)

    def bcast(src_ap, dst_ap, n, waits, bank=6):
        wf = bank_free[bank]
        v = p.op("pe", lambda e, s=src_ap, n=n, b=bank: e.matmul(
            pb[b][:, 0:n], s_ones[:], s, start=True, stop=True),
            waits=list(waits) + ([wf] if wf else []), inc=s_pe)
        bank_ready[bank] = (s_pe, v)
        v2 = p.op("dve", lambda e, d=dst_ap, n=n, b=bank: e.tensor_copy(d, pb[b][:, 0:n]),
                  waits=[(s_pe, v)], inc=s_ve)
        bank_free[bank] = (s_ve, v2)
        return (s_ve, v2)

    def stats_tile(tile_ap, i, base, waits, scratch_bf16, do_amax):
        p.op("act", lambda e, t=tile_ap, i=i, b=base, s=scratch_bf16: e.activation(
            s, t, AF.Square, accum_out=s_stats[:, b + 1, i:i + 1]), waits=waits)
        va = p.op("act", lambda e, t=tile_ap, i=i, b=base, s=scratch_bf16: e.activation(
            s, t, AF.Identity, accum_out=s_stats[:, b + 0, i:i + 1]), inc=s_ac)
        vd = None
        if do_amax:
            p.op("dve", lambda e, t=tile_ap, i=i, b=base: e.tensor_reduce(
                s_stats[:, b + 2, i:i + 1], t, AX.X, ALU.max), waits=waits)
            vd = p.op("dve", lambda e, t=tile_ap, i=i, b=base: e.tensor_reduce(
                s_stats[:, b + 3, i:i + 1], t, AX.X, ALU.min), inc=s_ve)
        return (s_ac, va), ((s_ve, vd) if vd is not None else None)

    def mr_cols(base, sl, waits):
        n = sl.stop - sl.start
        v1 = p.op("dve", lambda e, b=base, sl=sl: e.tensor_scalar(
            out=s_stats[:, b + 4, sl], in0=s_stats[:, b + 0, sl],
            scalar1=1.0 / D, scalar2=None, op0=ALU.mult), waits=waits, inc=s_ve)
        v2 = p.op("dve", lambda e, b=base, sl=sl: e.tensor_scalar(
            out=s_stats[:, b + 1, sl], in0=s_stats[:, b + 1, sl],
            scalar1=1.0 / D, scalar2=None, op0=ALU.mult), inc=s_ve)
        v3 = p.op("dve", lambda e, b=base, sl=sl, n=n: e.tensor_tensor(
            s_small[:, 0:n], s_stats[:, b + 4, sl], s_stats[:, b + 4, sl], ALU.mult),
            waits=[(s_ve, v1)], inc=s_ve)
        vv = p.op("dve", lambda e, b=base, sl=sl, n=n: e.tensor_tensor(
            s_stats[:, b + 1, sl], s_stats[:, b + 1, sl], s_small[:, 0:n], ALU.subtract),
            waits=[(s_ve, v2), (s_ve, v3)], inc=s_ve)
        vl = p.op("act", lambda e, b=base, sl=sl: e.activation(
            s_stats[:, b + 5, sl], s_stats[:, b + 1, sl], AF.Ln, bias=np.float32(EPS_LN), scale=1.0),
            waits=[(s_ve, vv)], inc=s_ac)
        va = p.op("act", lambda e, b=base, sl=sl: e.activation(
            s_stats[:, b + 5, sl], s_stats[:, b + 5, sl], AF.Exp, scale=-0.5),
            waits=[(s_ac, vl)], inc=s_ac)
        return (s_ac, va)

    def amax_cols(base, sl, waits):
        n = sl.stop - sl.start
        v1 = p.op("dve", lambda e, b=base, sl=sl, n=n: e.tensor_tensor(
            s_small[:, 0:n], s_stats[:, b + 2, sl], s_stats[:, b + 4, sl], ALU.subtract),
            waits=waits, inc=s_ve)
        v2 = p.op("dve", lambda e, b=base, sl=sl, n=n: e.tensor_tensor(
            s_small[:, 16:16 + n], s_stats[:, b + 4, sl], s_stats[:, b + 3, sl], ALU.subtract),
            inc=s_ve)
        v3 = p.op("dve", lambda e, b=base, sl=sl, n=n: e.tensor_tensor(
            s_small[:, 0:n], s_small[:, 0:n], s_small[:, 16:16 + n], ALU.max),
            waits=[(s_ve, v1), (s_ve, v2)], inc=s_ve)
        vv = p.op("dve", lambda e, b=base, sl=sl, n=n: e.tensor_tensor(
            s_stats[:, b + 6, sl], s_small[:, 0:n], s_stats[:, b + 5, sl], ALU.mult),
            waits=[(s_ve, v3)], inc=s_ve)
        return (s_ve, vv)

    def mr_tile(base, i, waits):
        v1 = p.op("dve", lambda e, i=i, b=base: e.tensor_scalar(
            out=s_stats[:, b + 4, i:i + 1], in0=s_stats[:, b + 0, i:i + 1],
            scalar1=1.0 / D, scalar2=None, op0=ALU.mult), waits=waits, inc=s_ve)
        v2 = p.op("dve", lambda e, i=i, b=base: e.tensor_scalar(
            out=s_stats[:, b + 1, i:i + 1], in0=s_stats[:, b + 1, i:i + 1],
            scalar1=1.0 / D, scalar2=None, op0=ALU.mult), inc=s_ve)
        v3 = p.op("dve", lambda e, i=i, b=base: e.tensor_tensor(
            s_small[:, 36:37], s_stats[:, b + 4, i:i + 1],
            s_stats[:, b + 4, i:i + 1], ALU.mult), waits=[(s_ve, v1)], inc=s_ve)
        vv = p.op("dve", lambda e, i=i, b=base: e.tensor_tensor(
            s_stats[:, b + 1, i:i + 1], s_stats[:, b + 1, i:i + 1],
            s_small[:, 36:37], ALU.subtract), waits=[(s_ve, v2), (s_ve, v3)], inc=s_ve)
        vl = p.op("act", lambda e, i=i, b=base: e.activation(
            s_stats[:, b + 5, i:i + 1], s_stats[:, b + 1, i:i + 1],
            AF.Ln, bias=np.float32(EPS_LN), scale=1.0), waits=[(s_ve, vv)], inc=s_ac)
        va = p.op("act", lambda e, i=i, b=base: e.activation(
            s_stats[:, b + 5, i:i + 1], s_stats[:, b + 5, i:i + 1],
            AF.Exp, scale=-0.5), waits=[(s_ac, vl)], inc=s_ac)
        vneg = p.op("dve", lambda e, i=i, b=base: e.scalar_tensor_tensor(
            out=s_small[:, 34 + (i % 2):35 + (i % 2)], in0=s_stats[:, b + 4, i:i + 1],
            scalar=-1.0, in1=s_stats[:, b + 5, i:i + 1], op0=ALU.mult, op1=ALU.mult),
            waits=[(s_ac, va)], inc=s_ve)
        return (s_ve, vneg)

    dma("gp", s_cols[:, 0, :], bq_d)
    dma("gp", s_cols[:, 2, :], bk_d)
    dma("gp", s_mask[:], mask_d)
    dma("gp", s_ident[:], ident_d)
    dma("gp", s_ones[:], ones_d)
    dma("gp", s_flag[:, 0:1], oddf_d)
    dma("gp", s_flag[:, 1:2], evenf_d)
    dma("gp", s_rowb[0:1, 0, :], bo_d)
    w_c0 = (s_gp, s_gp.n)
    p.op("dve", lambda e: e.tensor_scalar(out=s_cols[:, 1, :], in0=s_cols[:, 0, :],
                                          scalar1=-1.0, scalar2=None, op0=ALU.mult),
         waits=[w_c0])
    p.op("dve", lambda e: e.tensor_scalar(out=s_cols[:, 3, :], in0=s_cols[:, 2, :],
                                          scalar1=-1.0, scalar2=None, op0=ALU.mult))
    p.op("dve", lambda e: e.memset(s_onescol[:], 1.0))
    p.op("dve", lambda e: e.memset(s_ones8[:], 1.0))
    v_c = p.op("dve", lambda e: e.tensor_copy(s_onesb[:], s_ones[:]), inc=s_ve)
    w_const = (s_ve, v_c)

    X1A = [0, 4096, 8192]
    SCR_BF = 12288
    W1A = [16384, 20480]
    ROWS = 24576
    UQT = [12288, 16384]
    X1ST = 20480
    U2ST = 24576
    Q8A = [26624, 27648]
    FROW = 28672

    xfree = [None, None, None]
    last_st = []
    for i in range(NT):
        j = i % 3
        w = dma("sp", pool_f32(X1A[j], 2048), x0_d[i * 128:(i + 1) * 128, :],
                waits=[xfree[j]] if xfree[j] else [])
        (wa, wv) = stats_tile(pool_f32(X1A[j], 2048), i, ST1, [w],
                              pool_bf16(SCR_BF, 2048), True)
        xfree[j] = wv
        last_st = [wa, wv]
    wfree = [None, None]
    vwa = None
    for i in range(NT):
        j = i % 2
        w = dma("gp", pool_f32(W1A[j], 2048), wt_d[i * 128:(i + 1) * 128, :],
                waits=[wfree[j]] if wfree[j] else [])
        vwa = p.op("dve", lambda e, j=j, i=i: e.tensor_reduce(
            s_stats[:, STW, i:i + 1], pool_f32(W1A[j], 2048), AX.X, ALU.max,
            apply_absolute_value=True), waits=[w], inc=s_ve)
        wfree[j] = (s_ve, vwa)
    w_wpass1 = (s_ve, vwa)

    w_mr1 = mr_cols(ST1, slice(0, NT), last_st)
    w_am1 = amax_cols(ST1, slice(0, NT), [w_mr1, last_st[1]])
    row_scr = pool_f32(ROWS, 2048)
    w_ax = scalarize(s_stats[:, ST1 + 6, :], NT, 0, s_sc[0:1, 0:1], [w_am1], row_scr)
    w_aw = scalarize(s_stats[:, STW, :], NT, 1, s_sc[0:1, 1:2], [w_wpass1], row_scr)

    w1 = dma("gp", cc_am_in.ap()[0:1, 0:1], s_sc[0:1, 0:1], waits=[w_ax])
    p.op("po", lambda e: e.collective_compute(
        "AllReduce", ALU.max, replica_groups=[list(range(NCORE))],
        ins=[cc_am_in.ap().opt()], outs=[cc_am_out.ap().opt()]),
        waits=[w1], inc=s_c1)
    w2 = dma("gp", s_sc[0:1, 0:1], cc_am_out.ap()[0:1, 0:1], waits=[(s_c1, 1)])

    def ffn_scales(w_amax_ready, row_free_wait):
        wbf = dma("gp", pool_f32(ROWS, 2048)[0:1, :], bf_d,
                  waits=list(row_free_wait) if row_free_wait else [])
        v0 = p.op("dve", lambda e: e.tensor_tensor(s_sc[0:1, 6:7], s_sc[0:1, 0:1],
                                                   s_sc[0:1, 1:2], ALU.max),
                  waits=w_amax_ready, inc=s_ve)
        va1 = p.op("dve", lambda e: e.reciprocal(s_sc[0:1, 2:3], s_sc[0:1, 0:1]), inc=s_ve)
        va2 = p.op("dve", lambda e: e.reciprocal(s_sc[0:1, 3:4], s_sc[0:1, 6:7]),
                   waits=[(s_ve, v0)], inc=s_ve)
        va3 = p.op("dve", lambda e: e.tensor_scalar(out=s_sc[0:1, 2:3], in0=s_sc[0:1, 2:3],
                                                    scalar1=FP8_HALF_SCALE, scalar2=None, op0=ALU.mult),
                   waits=[(s_ve, va1)], inc=s_ve)
        va4 = p.op("dve", lambda e: e.tensor_scalar(out=s_sc[0:1, 3:4], in0=s_sc[0:1, 3:4],
                                                    scalar1=FP8_HALF_SCALE, scalar2=None, op0=ALU.mult),
                   waits=[(s_ve, va2)], inc=s_ve)
        va5 = p.op("dve", lambda e: e.tensor_tensor(s_sc[0:1, 4:5], s_sc[0:1, 0:1],
                                                    s_sc[0:1, 6:7], ALU.mult),
                   waits=[(s_ve, v0)], inc=s_ve)
        va6 = p.op("dve", lambda e: e.tensor_scalar(out=s_sc[0:1, 4:5], in0=s_sc[0:1, 4:5],
                                                    scalar1=DEQ_CONST, scalar2=None, op0=ALU.mult),
                   waits=[(s_ve, va5)], inc=s_ve)
        v = p.op("dve", lambda e: e.reciprocal(s_sc[0:1, 5:6], s_sc[0:1, 4:5]),
                 waits=[(s_ve, va6)], inc=s_ve)
        vrow = p.op("dve", lambda e: e.tensor_scalar(
            out=pool_f32(ROWS, 2048)[0:1, :], in0=pool_f32(ROWS, 2048)[0:1, :],
            scalar1=s_sc[0:1, 5:6], scalar2=0.5, op0=ALU.mult, op1=ALU.mult),
            waits=[(s_ve, v), wbf], inc=s_ve)
        vb = p.op("dve", lambda e: e.tensor_copy(pool_bf16(FROW, 2048)[0:1, :], pool_f32(ROWS, 2048)[0:1, :]),
                  waits=[(s_ve, vrow)], inc=s_ve)
        wq1 = bcast(s_sc[0:1, 2:3], s_bc[:, 0:1], 1, [(s_ve, vb), (s_ve, va3), (s_ve, va4)])
        wq2 = bcast(s_sc[0:1, 4:5], s_bc[:, 1:2], 1, [wq1])
        wq3 = bcast(s_sc[0:1, 3:4], s_bc[:, 2:3], 1, [wq2])
        return wq3

    w_sc1 = ffn_scales([w2, w_aw, w_const], None)

    def w_quant_pass(w_scale_ready, gate, a1_war):
        wfree2 = [gate, gate]
        qact = [None, None]
        last = None
        for i in range(NT):
            j = i % 2
            w = dma("sp", pool_f32(W1A[j], 2048), wt_d[i * 128:(i + 1) * 128, :],
                    waits=[wfree2[j]] if wfree2[j] else [])
            vq = p.op("dve", lambda e, j=j: e.tensor_scalar(
                out=pool_fp8(Q8A[j], 2048), in0=pool_f32(W1A[j], 2048),
                scalar1=s_bc[:, 2:3], scalar2=None, op0=ALU.mult),
                waits=[w, w_scale_ready] + ([qact[j]] if qact[j] else []), inc=s_ve)
            wfree2[j] = (s_ve, vq)
            last = p.op("act", lambda e, i=i, j=j: e.copy(a1_t(i), pool_fp8(Q8A[j], 2048)),
                        waits=[(s_ve, vq)] + ([a1_war] if a1_war else []), inc=s_ac)
            qact[j] = (s_ac, last)
        return (s_ac, last)

    w_wq = w_quant_pass(w_sc1, w_wpass1, None)

    def ffn_pass(src_d, dst_d, stats_base, out_stats_base, w_wq_ready, w_bc_ready,
                 produce_u2, src_ready, u2_dst=None):
        X4 = [0, 4096, 8192, 20480]
        xfree4 = [[src_ready] if src_ready else []] * 4
        xfree4 = [list(x) for x in xfree4]
        uqt_free = [None, None]
        u2_free = None
        q8_free = [None, None]
        w_last = [None]
        uq_written = {}

        def stage1(g):
            ws = []
            for i in (g * 2, g * 2 + 1):
                j = i % 4
                w = dma("sp", pool_f32(X4[j], 2048), src_d[i * 128:(i + 1) * 128, :],
                        waits=xfree4[j])
                vs1 = p.op("dve", lambda e, i=i, b=stats_base: e.tensor_scalar(
                    out=s_small[:, 32:33], in0=s_stats[:, b + 5, i:i + 1],
                    scalar1=s_bc[:, 0:1], scalar2=None, op0=ALU.mult),
                    waits=[w, w_bc_ready], inc=s_ve)
                vs2 = p.op("dve", lambda e, i=i, b=stats_base: e.scalar_tensor_tensor(
                    out=s_small[:, 33:34], in0=s_stats[:, b + 4, i:i + 1],
                    scalar=-1.0, in1=s_small[:, 32:33], op0=ALU.mult, op1=ALU.mult),
                    waits=[(s_ve, vs1)], inc=s_ve)
                jq = i % 2
                vq = p.op("dve", lambda e, j=j, jq=jq: e.tensor_scalar(
                    out=pool_fp8(Q8A[jq], 2048), in0=pool_f32(X4[j], 2048),
                    scalar1=s_small[:, 32:33], scalar2=s_small[:, 33:34],
                    op0=ALU.mult, op1=ALU.add),
                    waits=[(s_ve, vs2)] + ([q8_free[jq]] if q8_free[jq] else []), inc=s_ve)
                wuq = dma("gp", UQ_d[i * 128:(i + 1) * 128, :], pool_fp8(Q8A[jq], 2048),
                          waits=[(s_ve, vq)])
                q8_free[jq] = wuq
                ws.append(wuq)
            uq_written[g] = ws

        def stage2(g):
            nonlocal u2_free
            tiles = [g * 2, g * 2 + 1]
            ju = g % 2
            uqt_ready = None
            for kc in range(KC):
                wt = dma("sp", pool_bf16(UQT[ju] + kc * 256, 256),
                         UQ_d[g * 256:(g + 1) * 256, kc * 128:(kc + 1) * 128],
                         waits=uq_written[g] + ([uqt_free[ju]] if (uqt_free[ju] and kc == 0) else []),
                         transpose=True)
                uqt_ready = wt
            vmm = None
            for idx, i in enumerate(tiles):
                j = i % 4
                vev = None
                for oc in range(4):
                    b = (i * 4 + oc) % 4
                    wfb = bank_free[b]
                    for kc in range(KC):
                        lhs = pool_bf16(UQT[ju] + kc * 256 + idx * 128, 128)
                        ww = ([uqt_ready, w_wq_ready] + ([wfb] if wfb else [])) if kc == 0 else []
                        p.op("pe", lambda e, lhs=lhs, kc=kc, oc=oc, b=b: e.matmul(
                            pb[b][:, 0:512], lhs, a1_t(kc)[:, oc * 512:(oc + 1) * 512],
                            start=(kc == 0), stop=False), waits=ww)
                    vmm = p.op("pe", lambda e, oc=oc, b=b: e.matmul(
                        pb[b][:, 0:512], s_onesb[:], pool_bf16(FROW + oc * 512, 512)[0:1, :],
                        start=False, stop=True), inc=s_pe)
                    bank_ready[b] = (s_pe, vmm)
                    vev = p.op("dve", lambda e, b=b, oc=oc, j=j: e.scalar_tensor_tensor(
                        out=pool_f32(X4[j], 2048)[:, oc * 512:(oc + 1) * 512],
                        in0=pb[b][:, 0:512], scalar=s_bc[:, 1:2],
                        in1=pool_f32(X4[j], 2048)[:, oc * 512:(oc + 1) * 512],
                        op0=ALU.mult, op1=ALU.add),
                        waits=[(s_pe, vmm)], inc=s_ve)
                    bank_free[b] = (s_ve, vev)
                w_ev = (s_ve, vev)
                if out_stats_base is not None:
                    (wa2, wv2) = stats_tile(pool_f32(X4[j], 2048), i, out_stats_base,
                                            [w_ev], a2_t(0),
                                            do_amax=(out_stats_base == ST3))
                    stw = [wa2] + ([wv2] if wv2 else [])
                else:
                    stw = []
                wst = dma("gp", dst_d[i * 128:(i + 1) * 128, :], pool_f32(X4[j], 2048),
                          waits=[w_ev] + stw)
                w_last[0] = wst
                slot_frees = [wst] + stw
                if produce_u2:
                    wmr = mr_tile(out_stats_base, i, stw)
                    vu2 = p.op("dve", lambda e, i=i, j=j, b=out_stats_base: e.tensor_scalar(
                        out=pool_bf16(U2ST, 2048), in0=pool_f32(X4[j], 2048),
                        scalar1=s_stats[:, b + 5, i:i + 1],
                        scalar2=s_small[:, 34 + (i % 2):35 + (i % 2)],
                        op0=ALU.mult, op1=ALU.add),
                        waits=[wmr] + ([u2_free] if u2_free else []), inc=s_ve)
                    wu2 = dma("gp", u2_dst[i * 128:(i + 1) * 128, :], pool_bf16(U2ST, 2048),
                              waits=[(s_ve, vu2)])
                    u2_free = wu2
                    slot_frees += [wu2, (s_ve, vu2)]
                xfree4[j] = slot_frees
            uqt_free[ju] = (s_pe, vmm)

        stage1(0)
        stage1(1)
        for g in range(8):
            stage2(g)
            if g + 2 < 8:
                stage1(g + 2)
        return w_last[0]

    ffn1_dst = out_d if upto == 1 else X1_d
    w_x1_done = ffn_pass(x0_d, ffn1_dst, ST1, ST2, w_wq, w_sc1,
                         produce_u2=(upto >= 2), src_ready=None, u2_dst=U2_d)
    w_u2_all = (s_gp, s_gp.n)
    w_ffn1_pe = (s_pe, s_pe.n)

    def _rest():
        if upto >= 2:
            WQS = [0, 8192]
            ELU1 = 16384
            ELU3 = 18432
            QFST = [20480, 20992]
            KTM = [21504, 23552]
            VTS = [25600, 27664]
            VSTG = [16384, 16912]
            BVROW = 17440

            w_ffn1_ac = (s_ac, s_ac.n)
            u2fm_last = None
            for kc in range(KC):
                u2fm_last = dma("sp", a2_t(kc), U2_d[0:T, kc * 128:(kc + 1) * 128],
                                waits=[w_u2_all, w_ffn1_ac], transpose=True)
            w_u2fm = u2fm_last

            elu_prev_dve = [None]

            def elu_evac(bank, bias_col, nbias_col, dst_bf16, ready, dst_war=None, extra_aw=None):
                aw = [ready] + ([elu_prev_dve[0]] if elu_prev_dve[0] else []) + (extra_aw or [])
                p.op("act", lambda e, b=bank, bc=bias_col: e.activation(
                    pool_f32(ELU1, 512), pb[b][:, 0:512], AF.Relu, bias=bc, scale=1.0),
                    waits=aw)
                vr2 = p.op("act", lambda e, b=bank, nbc=nbias_col: e.activation(
                    pool_f32(ELU3, 512), pb[b][:, 0:512], AF.Relu, bias=nbc, scale=-1.0),
                    inc=s_ac)
                va = p.op("act", lambda e: e.activation(
                    pool_f32(ELU3, 512), pool_f32(ELU3, 512), AF.Exp, scale=-1.0),
                    waits=[(s_ac, vr2)], inc=s_ac)
                vd = p.op("dve", lambda e, d=dst_bf16: e.tensor_tensor(
                    d, pool_f32(ELU1, 512), pool_f32(ELU3, 512), ALU.add),
                    waits=[(s_ac, va)] + ([dst_war] if dst_war else []), inc=s_ve)
                elu_prev_dve[0] = (s_ve, vd)
                return (s_ve, vd)

            w_ffn1_all = [(s_ve, s_ve.n), (s_ac, s_ac.n), (s_pe, s_pe.n), (s_gp, s_gp.n)]
            wq_free = [w_ffn1_all, w_ffn1_all]
            w_kf_all = None
            for q in range(4):
                jw = q % 2
                wwq = dma("gp", pool_bf16(WQS[jw], 8192).rearrange("p (kc n) -> p kc n", kc=KC),
                          wk_d.rearrange("(kc p) n -> p kc n", p=128)[:, :, q * 512:(q + 1) * 512],
                          waits=(wq_free[jw] if isinstance(wq_free[jw], list)
                                 else ([wq_free[jw]] if wq_free[jw] else [])))
                vmm = None
                for dl in range(4):
                    dkc = q * 4 + dl
                    for tb in range(4):
                        b = (dkc * 4 + tb) % 2
                        wfb = bank_free[b]
                        for kc in range(KC):
                            ww = ([wwq, w_u2fm] + ([wfb] if wfb else [])) if kc == 0 else []
                            last = (kc == KC - 1)
                            if last:
                                vmm = p.op("pe", lambda e, kc=kc, dl=dl, tb=tb, b=b, jw=jw: e.matmul(
                                    pb[b][:, 0:512],
                                    pool_bf16(WQS[jw] + kc * 512 + dl * 128, 128),
                                    a2_t(kc)[:, tb * 512:(tb + 1) * 512],
                                    start=False, stop=True), waits=ww, inc=s_pe)
                            else:
                                p.op("pe", lambda e, kc=kc, dl=dl, tb=tb, b=b, jw=jw: e.matmul(
                                    pb[b][:, 0:512],
                                    pool_bf16(WQS[jw] + kc * 512 + dl * 128, 128),
                                    a2_t(kc)[:, tb * 512:(tb + 1) * 512],
                                    start=(kc == 0), stop=False), waits=ww)
                        bank_ready[b] = (s_pe, vmm)
                        wkf = elu_evac(b, s_cols[:, 2, dkc:dkc + 1], s_cols[:, 3, dkc:dkc + 1],
                                       a1_t(dkc)[:, tb * 512:(tb + 1) * 512], (s_pe, vmm),
                                       dst_war=w_ffn1_pe)
                        bank_free[b] = wkf
                    wkfs = dma("gp", KF_d[dkc * 128:(dkc + 1) * 128, :], a1_t(dkc),
                               waits=[bank_free[0], bank_free[1]])
                    w_kf_all = wkfs
                wq_free[jw] = (s_pe, vmm)

            if astage < 2:
                dma("gp", out_d, X1_d, waits=[w_x1_done, (s_ve, s_ve.n), (s_ac, s_ac.n), (s_pe, s_pe.n), (s_gp, s_gp.n), (s_ld, s_ld.n)])
                raise _StageStop
            vt_free = [None, None]
            vt_stored = [None] * NT
            vst_ready = [None, None]
            w_passA_act = (s_ac, s_ac.n)
            w_bvrow = dma("gp", pool_bf16(BVROW, 2048)[0:1, :], bv_d,
                          waits=[(s_pe, s_pe.n)])
            for j in range(2):
                vv = p.op("dve", lambda e, j=j: e.memset(pool_bf16(VSTG[j], 516), 1.0),
                          waits=[w_ffn1_pe, w_passA_act], inc=s_ve)
                vst_ready[j] = (s_ve, vv)
            for q in range(4):
                jw = q % 2
                wwq = dma("gp", pool_bf16(WQS[jw], 8192).rearrange("p (kc n) -> p kc n", kc=KC),
                          wv_d.rearrange("(kc p) n -> p kc n", p=128)[:, :, q * 512:(q + 1) * 512],
                          waits=[wq_free[jw]] if wq_free[jw] else [])
                vmm = None
                for tt in range(NT):
                    b = (q * NT + tt) % 2
                    wfb = bank_free[b]
                    for kc in range(KC):
                        ww = ([wwq, w_u2fm] + ([wfb] if wfb else [])) if kc == 0 else []
                        p.op("pe", lambda e, kc=kc, tt=tt, b=b, jw=jw: e.matmul(
                            pb[b][:, 0:512],
                            a2_t(kc)[:, tt * 128:(tt + 1) * 128],
                            pool_bf16(WQS[jw] + kc * 512, 512),
                            start=(kc == 0), stop=False), waits=ww)
                    vmm = p.op("pe", lambda e, b=b, q=q: e.matmul(
                        pb[b][:, 0:512], s_onesb[:], pool_bf16(BVROW + q * 512, 512)[0:1, :],
                        start=False, stop=True), waits=[w_bvrow], inc=s_pe)
                    bank_ready[b] = (s_pe, vmm)
                    jv = tt % 2
                    vv = None
                    for hh in range(4):
                        vv = p.op("dve", lambda e, b=b, hh=hh, jv=jv: e.tensor_copy(
                            pool_bf16(VSTG[jv] + hh * 129, 128),
                            pb[b][:, hh * 128:(hh + 1) * 128]),
                            waits=[(s_pe, vmm), vst_ready[jv]] +
                                  ([vt_free[jv]] if (vt_free[jv] and hh == 0) else []),
                            inc=(s_ve if hh == 3 else None))
                    bank_free[b] = (s_ve, vv)
                    wvt = dma("gp", VT_d[tt, q, :, :], pool_bf16(VSTG[jv], 516),
                              waits=[(s_ve, vv)])
                    vt_free[jv] = wvt
                    vt_stored[tt] = wvt
                wq_free[jw] = (s_pe, vmm)
            w_vt_all = (s_gp, s_gp.n)

            if astage < 3:
                dma("gp", out_d, X1_d, waits=[w_x1_done, (s_ve, s_ve.n), (s_ac, s_ac.n), (s_pe, s_pe.n), (s_gp, s_gp.n), (s_ld, s_ld.n)])
                raise _StageStop
            w_a2_gp = (s_gp, s_gp.n)
            elu_prev_dve[0] = (s_ve, s_ve.n)
            qf_free = [w_a2_gp, w_a2_gp]

            ktm_free = [None, None]
            vtb_free = [None, None]
            kv_last = {}
            ks_last = None
            for q in range(4):
                jw = q % 2
                wwq = dma("gp", pool_bf16(WQS[jw], 8192).rearrange("p (kc n) -> p kc n", kc=KC),
                          wq_d.rearrange("(kc p) n -> p kc n", p=128)[:, :, q * 512:(q + 1) * 512],
                          waits=[wq_free[jw]] if wq_free[jw] else [])
                vmm = None
                for dl in range(4):
                    dqc = q * 4 + dl
                    for tb in range(4):
                        b = (dqc * 4 + tb) % 2
                        wfb = bank_free[b]
                        for kc in range(KC):
                            ww = ([wwq, w_u2fm] + ([wfb] if wfb else [])) if kc == 0 else []
                            last = (kc == KC - 1)
                            if last:
                                vmm = p.op("pe", lambda e, kc=kc, dl=dl, tb=tb, b=b, jw=jw: e.matmul(
                                    pb[b][:, 0:512],
                                    pool_bf16(WQS[jw] + kc * 512 + dl * 128, 128),
                                    a2_t(kc)[:, tb * 512:(tb + 1) * 512],
                                    start=False, stop=True), waits=ww, inc=s_pe)
                            else:
                                p.op("pe", lambda e, kc=kc, dl=dl, tb=tb, b=b, jw=jw: e.matmul(
                                    pb[b][:, 0:512],
                                    pool_bf16(WQS[jw] + kc * 512 + dl * 128, 128),
                                    a2_t(kc)[:, tb * 512:(tb + 1) * 512],
                                    start=(kc == 0), stop=False), waits=ww)
                        bank_ready[b] = (s_pe, vmm)
                        jf = (dqc * 4 + tb) % 2
                        wqf = elu_evac(b, s_cols[:, 0, dqc:dqc + 1], s_cols[:, 1, dqc:dqc + 1],
                                       pool_bf16(QFST[jf], 512), (s_pe, vmm),
                                       dst_war=(qf_free[jf] if qf_free[jf] else None),
                                       extra_aw=[w_a2_gp] if (dqc == 0 and tb == 0) else None)
                        bank_free[b] = wqf
                        wqs = dma("gp", QF_d[dqc * 128:(dqc + 1) * 128, tb * 512:(tb + 1) * 512],
                                  pool_bf16(QFST[jf], 512), waits=[wqf])
                        qf_free[jf] = wqs
                wq_free[jw] = (s_pe, vmm)
                for tt in (range(NT) if (q == 0 and astage != 31 and not os.environ.get("KVT_OFF")) else []):
                    jk = tt % 2
                    if os.environ.get("KVT_PLAIN"):
                        wk1 = dma("sp", pool_bf16(KTM[jk], 2048), KF_d[tt * 128:(tt + 1) * 128, 0:T][0:128, 0:2048],
                                  waits=[w_kf_all] + ([ktm_free[jk]] if ktm_free[jk] else []))
                    else:
                        wk1 = dma("sp", pool_bf16(KTM[jk], 2048), KF_d[0:T, tt * 128:(tt + 1) * 128],
                                  waits=[w_kf_all] + ([ktm_free[jk]] if ktm_free[jk] else []),
                                  transpose=True)
                    wv1 = dma("sp", pool_bf16(VTS[jk], 2064).rearrange("p (qq n) -> p qq n", qq=4),
                              VT_d[tt, :, :, :].rearrange("qq p e -> p qq e"),
                              waits=[vt_stored[tt]] + ([vtb_free[jk]] if vtb_free[jk] else []))
                    for h in range(H):
                        bb = 2 + h // 4
                        wfb = bank_free[bb] if (tt == 0 and h % 4 == 0) else None
                        vkv = p.op("pe", lambda e, bb=bb, h=h, jk=jk, tt=tt: e.matmul(
                            pb[bb][:, (h % 4) * 128:(h % 4) * 128 + 128],
                            pool_bf16(KTM[jk] + h * 128, 128),
                            pool_bf16(VTS[jk] + (h // 4) * 516 + (h % 4) * 129, 128),
                            start=(tt == 0 and h % 4 == 0),
                            stop=(tt == NT - 1 and h % 4 == 3), skip_group_check=True),
                            waits=[wk1, wv1] + ([wfb] if wfb else []),
                            inc=(s_pe if tt == NT - 1 else None))
                        wfb6 = bank_free[6] if (tt == 0 and h == 0) else None
                        vks = p.op("pe", lambda e, h=h, jk=jk, tt=tt: e.matmul(
                            pb[6][:, h:h + 1],
                            pool_bf16(KTM[jk] + h * 128, 128),
                            s_onescol[:],
                            start=(tt == 0 and h == 0),
                            stop=(tt == NT - 1 and h == H - 1), skip_group_check=True),
                            waits=([wfb6] if wfb6 else []) + [w_const],
                            inc=(s_pe if (tt == NT - 1 or h == H - 1) else None))
                        if tt == NT - 1:
                            kv_last[h] = (s_pe, vkv)
                            ks_last = (s_pe, vks)
                        if h == H - 1:
                            ktm_free[jk] = (s_pe, vks)
                            vtb_free[jk] = (s_pe, vks)
                if q == 0 and tt == NT - 1 and astage != 31:
                    if astage in (31, 32):
                        dma("gp", out_d, X1_d, waits=[w_x1_done, (s_ve, s_ve.n), (s_ac, s_ac.n), (s_pe, s_pe.n), (s_gp, s_gp.n), (s_ld, s_ld.n)])
                        raise _StageStop

                    vst = None
                    for h in range(H):
                        bb = 2 + h // 4
                        vst = p.op("dve", lambda e, bb=bb, h=h: e.tensor_copy(
                            s_st[:, h, 0:128], pb[bb][:, (h % 4) * 128:(h % 4) * 128 + 128]),
                            waits=[kv_last[(h // 4) * 4 + 3]], inc=s_ve)
                        bank_free[bb] = (s_ve, vst)
                    vst = p.op("dve", lambda e: e.tensor_copy(
                        s_st[:, :, 128], pb[6][:, 0:16]), waits=[ks_last], inc=s_ve)
                    bank_free[6] = (s_ve, vst)
                    w_tot = (s_ve, vst)

                    if astage == 33:
                        dma("gp", out_d, X1_d, waits=[w_x1_done, w_tot, (s_ve, s_ve.n), (s_ac, s_ac.n), (s_pe, s_pe.n), (s_gp, s_gp.n), (s_ld, s_ld.n)])
                        raise _StageStop
                    vcc = p.op("dve", lambda e: e.tensor_scalar(
                        out=s_st[:].rearrange("p h e -> p (h e)"),
                        in0=s_st[:].rearrange("p h e -> p (h e)"),
                        scalar1=s_flag[:, 1:2], scalar2=None, op0=ALU.mult),
                        waits=[w_tot], inc=s_ve)
                    wcc1 = dma("sp", cc_kv_in.ap().rearrange("(h p) e -> p h e", p=128), s_st[:],
                               waits=[(s_ve, vcc)])
                    p.op("po", lambda e: e.collective_compute(
                        "AllReduce", ALU.add, replica_groups=[[0, 1], [2, 3], [4, 5], [6, 7]],
                        ins=[cc_kv_in.ap().opt()], outs=[cc_kv_out.ap().opt()]),
                        waits=[wcc1], inc=s_c2)
                    wcc2 = dma("sp", s_st[:], cc_kv_out.ap().rearrange("(h p) e -> p h e", p=128),
                               waits=[(s_c2, 1)])
                    vst0 = p.op("dve", lambda e: e.tensor_scalar(
                        out=s_st[:].rearrange("p h e -> p (h e)"),
                        in0=s_st[:].rearrange("p h e -> p (h e)"),
                        scalar1=s_flag[:, 0:1], scalar2=None, op0=ALU.mult),
                        waits=[wcc2], inc=s_ve)
                    w_st_init = (s_ve, vst0)

                    if astage < 4:
                        dma("gp", out_d, X1_d, waits=[w_x1_done, w_st_init, (s_ve, s_ve.n), (s_ac, s_ac.n), (s_pe, s_pe.n), (s_gp, s_gp.n), (s_ld, s_ld.n)])
                        raise _StageStop

            w_qf_all = (s_gp, s_gp.n)
            w_b_pe = (s_pe, s_pe.n)
            w_b_ac = (s_ac, s_ac.n)
            w_b_ve = (s_ve, s_ve.n)
            wo_ready = None
            for kc in range(KC):
                wo_ready = dma("gp", a2_t(kc), wo_d[kc * 128:(kc + 1) * 128, :],
                               waits=[w_b_pe])
            w_wo = wo_ready

            QFC = [0, 2048]
            KTC = [4096, 6144]
            VTC = [8192, 10256]
            X1C = [12320, 16416]
            X2ST = 20512
            SMO = 24608
            OFM = 26656
            STB = 28704

            qfc_free = [None, None]
            ktc_free = [None, None]
            vtc_free = [None, None]
            x1c_free = [None, None]
            x2_free = None
            ofm_free = None
            prev_updates = [w_st_init]
            SGRP = [[0, 1, 2, 3], [4, 5, 6, 7], [8, 9, 10, 11], [12, 13, 14, 15]]
            KVGRP = [[0, 1, 2], [3, 4, 5], [6, 7, 8], [9, 10, 11], [12, 13, 14], [15]]
            OGRP = [(4, [0, 1, 2]), (5, [3, 4, 5]), (4, [6, 7, 8]), (5, [9, 10, 11]),
                    (4, [12, 13, 14]), (5, [15])]

            chunk_w = {}

            def load_chunk(n):
                jq = n % 2
                gate = [w_b_pe, w_b_ac, w_b_ve] if n < 2 else []
                wq_ = dma("gp", pool_bf16(QFC[jq], 2048).rearrange("p (kc t) -> p kc t", kc=KC),
                          QF_d.rearrange("(kc p) t -> p kc t", p=128)[:, :, n * 128:(n + 1) * 128],
                          waits=[w_qf_all] + gate + ([qfc_free[jq]] if qfc_free[jq] else []))
                wk_ = dma("sp", pool_bf16(KTC[jq], 2048), KF_d[0:T, n * 128:(n + 1) * 128],
                          waits=[w_kf_all] + gate + ([ktc_free[jq]] if ktc_free[jq] else []),
                          transpose=True)
                wv_ = dma("gp", pool_bf16(VTC[jq], 2064).rearrange("p (qq nn) -> p qq nn", qq=4),
                          VT_d[n, :, :, :].rearrange("qq pp e -> pp qq e"),
                          waits=[w_vt_all] + gate + ([vtc_free[jq]] if vtc_free[jq] else []))
                wx_ = dma("sp", pool_f32(X1C[jq], 2048), X1_d[n * 128:(n + 1) * 128, :],
                          waits=[w_x1_done] + gate + ([x1c_free[jq]] if x1c_free[jq] else []))
                chunk_w[n] = (wq_, wk_, wv_, wx_)

            load_chunk(0)
            load_chunk(1)
            dst2_d = out_d if upto == 2 else X2_d
            w_x2_done = None
            for n in range(NT):
                jq = n % 2
                wq_, wk_, wv_, wx_ = chunk_w[n]
                va = p.op("act", lambda e: e.activation(
                    pool_bf16(STB, 2064),
                    s_st[:].rearrange("p h e -> p (h e)"),
                    AF.Identity, scale=1.0),
                    waits=prev_updates + [w_st_init], inc=s_ac)
                w_stb = (s_ac, va)
                prev_updates = []
                sm_ready = {}
                for gi, heads in enumerate(SGRP):
                    b = 2 + gi % 2
                    wfb = bank_free[b]
                    vmm = None
                    for ih, h in enumerate(heads):
                        vmm = p.op("pe", lambda e, b=b, ih=ih, h=h, jq=jq: e.matmul(
                            pb[b][:, ih * 128:(ih + 1) * 128],
                            pool_bf16(KTC[jq] + h * 128, 128),
                            pool_bf16(QFC[jq] + h * 128, 128),
                            start=(ih == 0), stop=(ih == 3), skip_group_check=True),
                            waits=([wk_, wq_] + ([wfb] if wfb else [])) if ih == 0 else [],
                            inc=(s_pe if ih == 3 else None))
                    bank_ready[b] = (s_pe, vmm)
                    vm = p.op("dve", lambda e, b=b, g=gi: e.tensor_tensor(
                        pool_bf16(SMO + g * 512, 512), pb[b][:, 0:512], s_mask[:], ALU.mult),
                        waits=[(s_pe, vmm)] + ([ofm_free] if (ofm_free and gi == 0) else []),
                        inc=s_ve)
                    bank_free[b] = (s_ve, vm)
                    for h in heads:
                        sm_ready[h] = (s_ve, vm)
                for gi, heads in enumerate(KVGRP):
                    b = 2 + gi % 2
                    wfb = bank_free[b]
                    vmm = None
                    for ih, h in enumerate(heads):
                        vmm = p.op("pe", lambda e, b=b, ih=ih, h=h, jq=jq, la=(h == heads[-1]): e.matmul(
                            pb[b][:, ih * 129:(ih + 1) * 129],
                            pool_bf16(KTC[jq] + h * 128, 128),
                            pool_bf16(VTC[jq] + (h // 4) * 516 + (h % 4) * 129, 129),
                            start=(ih == 0), stop=la, skip_group_check=True),
                            waits=([wv_, wk_] + ([wfb] if wfb else [])) if ih == 0 else [],
                            inc=(s_pe if h == heads[-1] else None))
                    bank_ready[b] = (s_pe, vmm)
                    nh = len(heads)
                    h0 = heads[0]
                    vu = p.op("dve", lambda e, b=b, h0=h0, nh=nh: e.tensor_tensor(
                        s_st[:, h0:h0 + nh, :].rearrange("p h e -> p (h e)"),
                        s_st[:, h0:h0 + nh, :].rearrange("p h e -> p (h e)"),
                        pb[b][:, 0:nh * 129], ALU.add),
                        waits=[(s_pe, vmm), w_stb], inc=s_ve)
                    bank_free[b] = (s_ve, vu)
                    prev_updates.append((s_ve, vu))
                o_scaled = {}
                for gi, (b, heads) in enumerate(OGRP):
                    wfb = bank_free[b]
                    vmm = None
                    for ih, h in enumerate(heads):
                        p.op("pe", lambda e, b=b, ih=ih, h=h, jq=jq: e.matmul(
                            pb[b][:, ih * 129:(ih + 1) * 129],
                            pool_bf16(QFC[jq] + h * 128, 128),
                            pool_bf16(STB + h * 129, 129),
                            start=(ih == 0), stop=False, skip_group_check=True),
                            waits=([w_stb] + ([wfb] if wfb else [])) if ih == 0 else [])
                        vmm = p.op("pe", lambda e, b=b, ih=ih, h=h, jq=jq, la=(h == heads[-1]): e.matmul(
                            pb[b][:, ih * 129:(ih + 1) * 129],
                            pool_bf16(SMO + h * 128, 128),
                            pool_bf16(VTC[jq] + (h // 4) * 516 + (h % 4) * 129, 129),
                            start=False, stop=la, skip_group_check=True),
                            waits=[sm_ready[h], wv_],
                            inc=(s_pe if h == heads[-1] else None))
                    bank_ready[b] = (s_pe, vmm)
                    nh = len(heads)
                    vze = None
                    for ih in range(nh):
                        vze = p.op("dve", lambda e, b=b, ih=ih: e.tensor_scalar(
                            out=s_small[:, 20 + ih:21 + ih],
                            in0=pb[b][:, ih * 129 + 128:ih * 129 + 129],
                            scalar1=EPS_Z, scalar2=None, op0=ALU.add),
                            waits=[(s_pe, vmm)] if ih == 0 else [],
                            inc=(s_ve if ih == nh - 1 else None))
                    vrz = p.op("dve", lambda e, nh=nh: e.reciprocal(
                        s_small[:, 24:24 + nh], s_small[:, 20:20 + nh]),
                        waits=[(s_ve, vze)], inc=s_ve)
                    vsc = None
                    for ih, h in enumerate(heads):
                        vsc = p.op("dve", lambda e, b=b, ih=ih, h=h: e.tensor_scalar(
                            out=pool_bf16(SMO + h * 128, 128),
                            in0=pb[b][:, ih * 129:ih * 129 + 128],
                            scalar1=s_small[:, 24 + ih:25 + ih], scalar2=None, op0=ALU.mult),
                            waits=[(s_ve, vrz), sm_ready[h]],
                            inc=(s_ve if h == heads[-1] else None))
                    bank_free[b] = (s_ve, vsc)
                    for h in heads:
                        o_scaled[h] = (s_ve, vsc)
                ofm_ready = []
                vof = None
                for tg in range(4):
                    wfb7 = bank_free[7]
                    vtp = None
                    for ih in range(4):
                        h = tg * 4 + ih
                        vtp = p.op("pe", lambda e, ih=ih, h=h: e.transpose(
                            pb7[:, ih * 128:(ih + 1) * 128],
                            pool_bf16(SMO + h * 128, 128), s_ident[:]),
                            waits=[o_scaled[h]] + ([wfb7] if (wfb7 and ih == 0) else []),
                            inc=(s_pe if ih == 3 else None))
                    bank_ready[7] = (s_pe, vtp)
                    vof = p.op("act", lambda e, tg=tg: e.copy(
                        pool_bf16(OFM + tg * 512, 512), pb7[:, 0:512]),
                        waits=[(s_pe, vtp)] + ([ofm_free] if (ofm_free and tg == 0) else []),
                        inc=s_ac)
                    bank_free[7] = (s_ac, vof)
                    ofm_ready.append((s_ac, vof))
                vx2 = None
                for oc in range(4):
                    b = (n * 4 + oc) % 2
                    wfb = bank_free[b]
                    for kc in range(KC):
                        ww = ([ofm_ready[3], w_wo] + ([wfb] if wfb else [])) if kc == 0 else []
                        p.op("pe", lambda e, kc=kc, oc=oc, b=b: e.matmul(
                            pb[b][:, 0:512], pool_bf16(OFM + kc * 128, 128),
                            a2_t(kc)[:, oc * 512:(oc + 1) * 512],
                            start=(kc == 0), stop=False), waits=ww)
                    vmm = p.op("pe", lambda e, oc=oc, b=b: e.matmul(
                        pb[b][:, 0:512], s_onesb[:], s_rowb[0:1, 0, oc * 512:(oc + 1) * 512],
                        start=False, stop=True), inc=s_pe)
                    bank_ready[b] = (s_pe, vmm)
                    vx2 = p.op("dve", lambda e, b=b, oc=oc, jq=jq: e.scalar_tensor_tensor(
                        out=pool_f32(X2ST, 2048)[:, oc * 512:(oc + 1) * 512],
                        in0=pb[b][:, 0:512], scalar=1.0,
                        in1=pool_f32(X1C[jq], 2048)[:, oc * 512:(oc + 1) * 512],
                        op0=ALU.mult, op1=ALU.add),
                        waits=[(s_pe, vmm), wx_] + ([x2_free] if (x2_free and oc == 0) else []),
                        inc=s_ve)
                    bank_free[b] = (s_ve, vx2)
                w_x2t = (s_ve, vx2)
                x1c_free[jq] = w_x2t
                (wa3, wv3) = stats_tile(pool_f32(X2ST, 2048), n, ST3, [w_x2t],
                                        pool_bf16(OFM, 2048), do_amax=True)
                ofm_free = wa3
                wst2 = dma("gp", dst2_d[n * 128:(n + 1) * 128, :], pool_f32(X2ST, 2048),
                           waits=[w_x2t, wa3] + ([wv3] if wv3 else []))
                x2_free = wst2
                wmr3 = mr_tile(ST3, n, [wa3] + ([wv3] if wv3 else []))
                amax_cols(ST3, slice(n, n + 1), [wmr3, wv3])
                qfc_free[jq] = (s_pe, s_pe.n)
                ktc_free[jq] = (s_pe, s_pe.n)
                vtc_free[jq] = (s_pe, s_pe.n)
                if n + 2 < NT:
                    load_chunk(n + 2)
                w_x2_done = wst2

        if upto >= 3:
            w_am3 = (s_ve, s_ve.n)
            w_2d_gp = (s_gp, s_gp.n)
            w_2d_pe = (s_pe, s_pe.n)
            w_2d_ve = (s_ve, s_ve.n)
            row_scr3 = pool_f32(ROWS, 2048)
            w_ax3 = scalarize(s_stats[:, ST3 + 6, :], NT, 2, s_sc[0:1, 0:1], [w_am3], row_scr3)
            w13 = dma("gp", cc_am_in2.ap()[0:1, 0:1], s_sc[0:1, 0:1], waits=[w_ax3])
            p.op("po", lambda e: e.collective_compute(
                "AllReduce", ALU.max, replica_groups=[list(range(NCORE))],
                ins=[cc_am_in2.ap().opt()], outs=[cc_am_out2.ap().opt()]),
                waits=[w13], inc=s_c3)
            w23 = dma("gp", s_sc[0:1, 0:1], cc_am_out2.ap()[0:1, 0:1], waits=[(s_c3, 1)])
            w_sc3 = ffn_scales([w23], [w_2d_ve, w_2d_pe])
            w_wq3 = w_quant_pass(w_sc3, w_2d_ve, w_2d_pe)
            ffn_pass(X2_d, out_d, ST3, None, w_wq3, w_sc3, produce_u2=False,
                     src_ready=w_x2_done)

        return
    try:
        _rest()
    except _StageStopCls:
        pass
    if dbg:
        o_stats = nc.dram_tensor("o_stats", [128, 24 * KC], F32, kind="ExternalOutput").ap()
        o_sc = nc.dram_tensor("o_sc", [1, 32], F32, kind="ExternalOutput").ap()
        o_bc = nc.dram_tensor("o_bc", [128, 8], F32, kind="ExternalOutput").ap()
        o_uq = nc.dram_tensor("o_uq", [T, D], BF16, kind="ExternalOutput").ap()
        o_x1 = nc.dram_tensor("o_x1", [T, D], F32, kind="ExternalOutput").ap()
        o_u2 = nc.dram_tensor("o_u2", [T, D], BF16, kind="ExternalOutput").ap()
        o_a1 = nc.dram_tensor("o_a1", [128, KC * D], BF16, kind="ExternalOutput").ap()
        wend = [(s_gp, s_gp.n), (s_ld, s_ld.n), (s_ve, s_ve.n), (s_ac, s_ac.n), (s_pe, s_pe.n)]
        dma("gp", o_stats, s_stats[:].rearrange("p a b -> p (a b)"), waits=wend)
        dma("gp", o_sc, s_sc[:], waits=[])
        dma("gp", o_bc, s_bc[:], waits=[])
        dma("gp", o_uq, UQ_d, waits=[])
        dma("gp", o_x1, X1_d, waits=[])
        dma("gp", o_u2, U2_d, waits=[])
        dma("gp", o_a1, A1[:].rearrange("p a b -> p (a b)"), waits=[])
        o_stt = nc.dram_tensor("o_stt", [128, H * 129], F32, kind="ExternalOutput").ap()
        dma("gp", o_stt, s_st[:].rearrange("p h e -> p (h e)"), waits=[])
    p.op("po", lambda e: None, waits=[(s_gp, s_gp.n), (s_ld, s_ld.n)])

    with es:
        with nc.Block() as block:
            def emitter(name):
                def f(eng):
                    for waits, fn, inc in p.ops[name]:
                        for h, v in waits:
                            eng.wait_ge(h, v)
                        inst = fn(eng)
                        if inc is not None:
                            if inst is None:
                                inst = eng.nop()
                            inst.then_inc(inc[0], inc[1])
                return f
            block.sync(emitter("sp"))
            block.scalar(emitter("act"))
            block.vector(emitter("dve"))
            block.tensor(emitter("pe"))
            block.gpsimd(emitter("po"))
    return nc


def _shard_inputs(L0_attn_x, L0_attn_Wq, L0_attn_bq, L0_attn_Wk, L0_attn_bk,
                  L0_attn_Wv, L0_attn_bv, L0_attn_Wo, L0_attn_bo,
                  L0_ffn_weight, L0_ffn_bias):
    import ml_dtypes
    xs = np.ascontiguousarray(np.asarray(L0_attn_x, dtype=np.float32)).reshape(16384, 2048)
    shared = dict(
        wq=np.ascontiguousarray(np.asarray(L0_attn_Wq, np.float32)),
        wk=np.ascontiguousarray(np.asarray(L0_attn_Wk, np.float32)),
        wv=np.ascontiguousarray(np.asarray(L0_attn_Wv, np.float32)),
        wo=np.ascontiguousarray(np.asarray(L0_attn_Wo, np.float32)),
        wt=np.ascontiguousarray(np.asarray(L0_ffn_weight, np.float32).T),
        bq_cols=np.ascontiguousarray(np.asarray(L0_attn_bq, np.float32).reshape(16, 128).T),
        bk_cols=np.ascontiguousarray(np.asarray(L0_attn_bk, np.float32).reshape(16, 128).T),
        bv_row=np.asarray(L0_attn_bv, np.float32).reshape(1, 2048),
        bo_row=np.asarray(L0_attn_bo, np.float32).reshape(1, 2048),
        bf_row=np.asarray(L0_ffn_bias, np.float32).reshape(1, 2048),
        ident=np.eye(128).astype(ml_dtypes.bfloat16),
        ones_row=np.ones((1, 128), np.float32),
    )
    m = (np.arange(128)[:, None] <= np.arange(128)[None, :]).astype(np.float32)
    shared["mask4"] = np.tile(m, (1, 4)).astype(ml_dtypes.bfloat16)
    in_maps = []
    for c in range(NCORE):
        d = dict(shared)
        d["x0"] = np.ascontiguousarray(xs[c * 2048:(c + 1) * 2048])
        d["oddflag"] = np.full((128, 1), float(c % 2), np.float32)
        d["evenflag"] = np.full((128, 1), float(1 - c % 2), np.float32)
        in_maps.append(d)
    return in_maps


def kernel(L0_attn_x, L0_attn_Wq, L0_attn_bq, L0_attn_Wk, L0_attn_bk,
           L0_attn_Wv, L0_attn_bv, L0_attn_Wo, L0_attn_bo,
           L0_ffn_weight, L0_ffn_bias):
    in_maps = _shard_inputs(L0_attn_x, L0_attn_Wq, L0_attn_bq, L0_attn_Wk,
                            L0_attn_bk, L0_attn_Wv, L0_attn_bv, L0_attn_Wo,
                            L0_attn_bo, L0_ffn_weight, L0_ffn_bias)
    nc = build_program(upto=3)
    res = run_bass_kernel_spmd(nc, in_maps, core_ids=list(range(NCORE)))
    out = np.concatenate([r["out"] for r in res.results], axis=0)
    return out.reshape(4, 4096, 2048)
